# version 06
# baseline (speedup 1.0000x reference)
import numpy as np
import ml_dtypes
from contextlib import ExitStack

import concourse.bass as bass
import concourse.mybir as mybir
from concourse.bass_utils import run_bass_kernel_spmd

F32 = mybir.dt.float32
BF16 = mybir.dt.bfloat16
AF = mybir.ActivationFunctionType
ALU = mybir.AluOpType
AX = mybir.AxisListType.X

T = 2048
DM = 2048
NT = 16
ND = 16
NH = 16
HD = 128
INTER = 1408
NI = 11
NU = 18
SCALE = 1.0 / (HD ** 0.5)
EPS = 1e-5

SUBLAYERS = ("attn_a", "moe", "attn_b", "moe")
DEV_DEBUG = False
DEV_MOE_DUMP = False
DEV_HEADS = NH
DEV_STOP = None
ENGS = ("pe", "act", "dve", "pool", "sync")


class Sem:
    def __init__(self, handle, name, owner=None):
        self.h = handle
        self.name = name
        self.owner = owner
        self.n = 0


class Builder:
    def __init__(self):
        self.ops = {k: [] for k in ENGS}
        self.waited = {k: {} for k in ENGS}

    def wait(self, eng, ck):
        if ck is None:
            return
        sem, val = ck
        if val <= 0:
            return
        assert val <= sem.n, f"wait on future: {sem.name} {val} > {sem.n}"
        if sem.owner == eng:
            return
        if self.waited[eng].get(sem, 0) >= val:
            return
        self.waited[eng][sem] = val
        self.ops[eng].append(lambda e, s=sem, v=val: e.wait_ge(s.h, v))

    def emit(self, eng, fn, waits=(), inc=None, amt=1):
        for ck in waits:
            self.wait(eng, ck)
        if inc is not None:
            inc.n += amt
            self.ops[eng].append(
                lambda e, f=fn, s=inc, a=amt: f(e).then_inc(s.h, a))
            return (inc, inc.n)
        self.ops[eng].append(fn)
        return None

    def drain(self, eng):
        self.ops[eng].append(lambda e: e.drain())

    def ck(self, sem):
        return (sem, sem.n)


class Bank:
    def __init__(self, ap):
        self.ap = ap
        self.writer = None
        self.readers = []


class BankPool:
    def __init__(self, b, banks):
        self.b = b
        self.banks = banks
        self.i = 0

    def acquire(self):
        bank = self.banks[self.i % len(self.banks)]
        self.i += 1
        return bank

    def pe_waits(self, bank):
        for ck in bank.readers:
            self.b.wait("pe", ck)
        bank.readers = []


def ceil_div(a, b):
    return -(-a // b)


def build_program(sublayers=SUBLAYERS):
    nc = bass.Bass("TRN2", target_bir_lowering=False,
                   detect_race_conditions=False)

    dx = nc.dram_tensor("x", [T, DM], F32, kind="ExternalInput")
    dy = nc.dram_tensor("y", [T, DM], F32, kind="ExternalOutput")
    d_wqkv = nc.dram_tensor("wqkvh", [NH, 128, ND * 384], F32, kind="ExternalInput")
    d_bqkv = nc.dram_tensor("bqkvt", [128, 48], F32, kind="ExternalInput")
    d_bvrow = nc.dram_tensor("bvrow", [1, DM], F32, kind="ExternalInput")
    d_wo_a = nc.dram_tensor("wo_a", [DM, DM], F32, kind="ExternalInput")
    d_bo = nc.dram_tensor("bo_bc", [128, DM], F32, kind="ExternalInput")
    d_wq_b = nc.dram_tensor("wqh", [NH, 128, ND * 128], F32, kind="ExternalInput")
    d_wkv_b = nc.dram_tensor("wkvb", [128, ND * 512], F32, kind="ExternalInput")
    d_wk_b = nc.dram_tensor("wkh", [NH, 128, 4 * 128], F32, kind="ExternalInput")
    d_wv_b = nc.dram_tensor("wvh", [NH, 128, 4 * 128], F32, kind="ExternalInput")
    d_wo_b = nc.dram_tensor("wo_b", [DM, DM], F32, kind="ExternalInput")
    d_gw = nc.dram_tensor("gwc", [128, ND * 16], F32, kind="ExternalInput")
    d_w1 = nc.dram_tensor("w1c", [16, NI, 128, ND * 128], F32, kind="ExternalInput")
    d_w3 = nc.dram_tensor("w3c", [16, NI, 128, ND * 128], F32, kind="ExternalInput")
    d_w2 = nc.dram_tensor("w2c", [16, 4, 128, NI * 512], F32, kind="ExternalInput")
    d_sw1 = nc.dram_tensor("sw1c", [2, NI, 128, ND * 128], F32, kind="ExternalInput")
    d_sw3 = nc.dram_tensor("sw3c", [2, NI, 128, ND * 128], F32, kind="ExternalInput")
    d_sw2 = nc.dram_tensor("sw2c", [2, 4, 128, NI * 512], F32, kind="ExternalInput")
    d_ident = nc.dram_tensor("ident", [128, 128], BF16, kind="ExternalInput")
    if DEV_DEBUG:
        d_dbg = {nm: nc.dram_tensor(f"dbg_{nm}", [128, T], BF16,
                                    kind="ExternalOutput")
                 for nm in ("qh", "kh", "vh", "oh", "probs", "probsT", "scores")}
    d_masks = nc.dram_tensor("masks", [128, 5 * 128], BF16, kind="ExternalInput")
    if DEV_MOE_DUMP:
        d_mxn = nc.dram_tensor("mdump_xn", [128, ND * 1024], BF16,
                               kind="ExternalOutput")
        d_mwg = nc.dram_tensor("mdump_wg", [128, 16 * 16], F32,
                               kind="ExternalOutput")
        d_mh = {u: nc.dram_tensor(f"mdump_h{u}", [128, NI * 1024], BF16,
                                  kind="ExternalOutput") for u in (0, 2)}

    es = ExitStack()

    def sb(name, shape, dt):
        return es.enter_context(nc.sbuf_tensor(name, shape, dt))[:]

    res = sb("res", [128, NT * DM], BF16)
    ident = sb("identsb", [128, 128], BF16)
    masks = sb("maskssb", [128, 5 * 128], BF16)
    bqkv = sb("bqkvsb", [128, 48], F32)
    gw = sb("gwsb", [128, ND * 16], BF16)
    wgate = sb("wgatesb", [128, NT * 16], F32)
    stats = sb("statssb", [128, 64], F32)
    gsc = sb("gscsb", [128, 96], F32)
    tiny = sb("tinysb", [128, 8], F32)
    sq_scr = sb("sqscr", [128, DM], BF16)

    fbanks = [Bank(es.enter_context(
        nc.psum_tensor(f"pf{i}", [128, 512], F32))[:]) for i in range(6)]
    bbanks = [Bank(es.enter_context(
        nc.psum_tensor(f"pb{i}", [128, 1024], BF16))[:]) for i in range(2)]

    sems = []

    def mksem(name, owner):
        h = es.enter_context(nc.semaphore(name))
        s = Sem(h, name, owner)
        sems.append(s)
        return s

    sPE = mksem("sPE", "pe")
    sACT = mksem("sACT", "act")
    sDVE = mksem("sDVE", "dve")
    sX = mksem("sX", None)
    sS = mksem("sS", None)
    s_bo = mksem("sBO", None)
    s_wA = [mksem("sWA0", None), mksem("sWA1", None)]
    s_wO = [mksem("sWO0", None), mksem("sWO1", None)]
    s_w13 = [mksem("sW13a", None), mksem("sW13b", None)]
    s_w2 = [mksem("sW2a", None), mksem("sW2b", None)]
    sOUT = [mksem("sOUT0", None), mksem("sOUT1", None)]

    with nc.Block() as b0:
        @b0.gpsimd
        def _(g):
            for s in sems:
                g.sem_clear(s.h)

    B = Builder()
    fpool = BankPool(B, fbanks)
    bpool = BankPool(B, bbanks)

    def res_tile(t):
        return res[:, t * DM:(t + 1) * DM]

    def res_slice(t, d0, d1):
        return res[:, t * DM + d0:t * DM + d1]

    for t in range(NT):
        B.emit("pool", lambda g, t=t: g.dma_start(
            res_tile(t), dx[t * 128:(t + 1) * 128, :]), inc=sX, amt=16)
    B.emit("sync", lambda s: s.dma_start(ident, d_ident[:, :]), inc=sS, amt=16)
    B.emit("sync", lambda s: s.dma_start(masks, d_masks[:, :]), inc=sS, amt=16)
    B.emit("sync", lambda s: s.dma_start(bqkv, d_bqkv[:, :]), inc=sS, amt=16)
    ck_gw = B.emit("pool", lambda g: g.dma_start(gw, d_gw[:, :]), inc=sX, amt=16)
    ck_x = B.ck(sX)
    res_ready = [ck_x] * NT
    ck_consts = B.ck(sS)

    def ln_stats(res_ready):
        for t in range(NT):
            B.emit("dve", lambda v, t=t: v.reduce_sum(
                stats[:, t:t + 1], res_tile(t), AX), waits=[res_ready[t]])
        B.drain("dve")
        ck_sq = None
        for t in range(NT):
            ck_sq = B.emit("act", lambda a, t=t: a.activation(
                sq_scr, res_tile(t), AF.Square,
                accum_out=stats[:, 16 + t:17 + t]),
                waits=[res_ready[t]], inc=sACT)
        B.emit("dve", lambda v: v.tensor_scalar(
            gsc[:, 0:16], stats[:, 0:16], 1.0 / DM, None, ALU.mult))
        B.drain("dve")
        B.emit("dve", lambda v: v.tensor_tensor(
            gsc[:, 16:32], gsc[:, 0:16], gsc[:, 0:16], ALU.mult))
        B.emit("dve", lambda v: v.tensor_scalar(
            gsc[:, 32:48], stats[:, 16:32], 1.0 / DM, EPS, ALU.mult, ALU.add),
            waits=[ck_sq])
        B.drain("dve")
        B.emit("dve", lambda v: v.tensor_tensor(
            gsc[:, 48:64], gsc[:, 32:48], gsc[:, 16:32], ALU.subtract))
        B.drain("dve")
        ck_rv = B.emit("dve", lambda v: v.reciprocal(
            gsc[:, 64:80], gsc[:, 48:64]), inc=sDVE)
        ck_r = B.emit("act", lambda a: a.activation(
            stats[:, 48:64], gsc[:, 64:80], AF.Sqrt), waits=[ck_rv], inc=sACT)
        ck_negmr = B.emit("dve", lambda v: v.scalar_tensor_tensor(
            stats[:, 32:48], gsc[:, 0:16], -1.0, stats[:, 48:64],
            ALU.mult, ALU.mult), waits=[ck_r], inc=sDVE)
        B.drain("dve")
        return ck_r, ck_negmr

    def make_xn_tile(dst, t, ck_r, ck_negmr, res_ck, guard):
        return B.emit("act", lambda a, dst=dst, t=t: a.activation(
            dst, res_tile(t), AF.Identity,
            bias=stats[:, 32 + t:33 + t], scale=stats[:, 48 + t:49 + t]),
            waits=[ck_r, ck_negmr, res_ck, guard], inc=sACT)

    def transpose_to_fm(src_tm, ck_tm, dst_fn, n_chunks=ND):
        last_eck = None
        for c0 in range(0, n_chunks, 4):
            bank = bpool.acquire()
            bpool.pe_waits(bank)
            gend = None
            for i in range(4):
                c = c0 + i
                gend = B.emit("pe", lambda p, c=c, i=i, bank=bank, src=src_tm:
                              p.transpose(
                    bank.ap[:, i * 128:(i + 1) * 128],
                    src[:, c * 128:(c + 1) * 128], ident),
                    waits=[ck_tm, ck_consts] if i == 0 else [],
                    inc=sPE if i == 3 else None)
            bank.writer = gend
            last_eck = B.emit("act", lambda a, bank=bank, c0=c0: a.activation(
                dst_fn(c0), bank.ap[:, 0:512], AF.Identity),
                waits=[gend], inc=sACT)
            bank.readers.append(last_eck)
        return last_eck

    def mm_group(bank, pairs, extra_waits=(), n_slice=512):
        fpool.pe_waits(bank)
        n = len(pairs)
        gend = None
        for i, (lf, rf) in enumerate(pairs):
            gend = B.emit("pe", lambda p, lf=lf, rf=rf, i=i, n=n, bank=bank,
                          ns=n_slice: p.matmul(
                bank.ap[:, 0:ns], lf(), rf(),
                start=(i == 0), stop=(i == n - 1)),
                waits=list(extra_waits) if i == 0 else [],
                inc=sPE if i == n - 1 else None)
        bank.writer = gend
        return gend

    def phase_fence():
        return [B.ck(sPE), B.ck(sACT), B.ck(sDVE)]

    def apply_fence(fence):
        for eng in ("pe", "act", "dve", "pool"):
            for f in fence:
                B.wait(eng, f)

    def attn_phase(kind, res_ready):
        ph = ExitStack()

        def psb(name, shape, dt):
            return ph.enter_context(nc.sbuf_tensor(name, shape, dt))[:]

        tag = kind
        xn = psb(f"xn{tag}", [128, ND * T], BF16)
        xn_tm = [psb(f"xntm{tag}{i}", [128, DM], BF16) for i in range(2)]
        SL = T if kind == "a" else 512
        if kind == "b":
            lat = psb("lat", [128, 4 * T], BF16)

        def xn_chunk(c):
            return xn[:, c * T:(c + 1) * T]

        xnv = xn.rearrange("p (c w) -> p c w", c=ND)

        ck_r, ck_negmr = ln_stats(res_ready)
        xn_guard = [None, None]
        for t in range(NT):
            buf = t % 2
            ck_tm = make_xn_tile(xn_tm[buf], t, ck_r, ck_negmr,
                                 res_ready[t], xn_guard[buf])
            transpose_to_fm(xn_tm[buf], ck_tm,
                            lambda c0, t=t: xnv[:, c0:c0 + 4, t * 128:(t + 1) * 128])
            xn_guard[buf] = B.ck(sPE)
        ck_xn = B.ck(sACT)
        if DEV_STOP == "xn":
            fence = phase_fence()
            ph.close()
            return res_ready, fence

        ck_lat = None
        lat_pe_done = None
        if kind == "b":
            wkv_scope = ExitStack()
            wkv_b = wkv_scope.enter_context(
                nc.sbuf_tensor("wkvbuf", [128, ND * 512], BF16))[:]
            ck_wkv = B.emit("pool", lambda g: g.dma_start(wkv_b, d_wkv_b[:, :]),
                            inc=s_bo, amt=16)
            for lc in range(4):
                for tb in range(4):
                    bank = fpool.acquire()
                    pairs = [(lambda c=c, lc=lc: wkv_b[:, c * 512 + lc * 128:
                                                       c * 512 + (lc + 1) * 128],
                              lambda c=c, tb=tb: xn_chunk(c)[:, tb * 512:(tb + 1) * 512])
                             for c in range(ND)]
                    gend = mm_group(bank, pairs, extra_waits=[ck_wkv, ck_xn])
                    eck = B.emit("act", lambda a, bank=bank, lc=lc, tb=tb:
                                 a.activation(
                        lat[:, lc * T + tb * 512:lc * T + (tb + 1) * 512],
                        bank.ap[:, 0:512], AF.Identity), waits=[gend], inc=sACT)
                    bank.readers.append(eck)
            ck_lat = B.ck(sACT)
            lat_pe_done = B.ck(sPE)
            wkv_scope.close()

        q_h = psb(f"qh{tag}", [128, T], BF16)
        k_h = psb(f"kh{tag}", [128, T], BF16)
        v_h = psb(f"vh{tag}", [128, NT * HD], BF16)
        out_h = psb(f"oh{tag}", [128, T], BF16)
        probs = [psb(f"pr{tag}{i}", [128, SL], BF16) for i in range(2)]
        probsT = [psb(f"prT{tag}{i}", [128, SL], BF16) for i in range(2)]
        dscr = sq_scr[:, 0:512]
        ones_row = sq_scr[0:1, 512:640]
        wo_b = [psb(f"wob{tag}{i}", [128, DM], BF16) for i in range(2)]
        if kind == "a":
            wqkv_b = [psb(f"wqkvb{i}", [128, ND * 384], BF16) for i in range(2)]
            bo_bc = xn_tm[0]
            bv_row = xn_tm[1][0:1, 0:DM]
            ck_bv = B.emit("pool", lambda g: g.dma_start(
                bv_row, d_bvrow[:, :]), waits=[xn_guard[0], xn_guard[1]],
                inc=s_bo, amt=16)
        else:
            wq_b = [psb(f"wqb{i}", [128, ND * 128], BF16) for i in range(2)]
            wk_b = [psb(f"wkb{i}", [128, 4 * 128], BF16) for i in range(2)]
            wv_b = [psb(f"wvb{i}", [128, 4 * 128], BF16) for i in range(2)]
            ohT = [psb(f"ohT{i}", [128, T], BF16) for i in range(2)]
            ck_bv = None

        def issue_head_dma(h, slot, w_war, o_war):
            if kind == "a":
                ckw = B.emit("pool", lambda g, h=h, slot=slot: g.dma_start(
                    wqkv_b[slot], d_wqkv[h]), waits=[w_war],
                    inc=s_wA[slot], amt=16)
            else:
                B.emit("pool", lambda g, h=h, slot=slot: g.dma_start(
                    wq_b[slot], d_wq_b[h]), waits=[w_war, lat_pe_done],
                    inc=s_wA[slot], amt=16)
                B.emit("pool", lambda g, h=h, slot=slot: g.dma_start(
                    wk_b[slot], d_wk_b[h]), inc=s_wA[slot], amt=16)
                ckw = B.emit("pool", lambda g, h=h, slot=slot: g.dma_start(
                    wv_b[slot], d_wv_b[h]), inc=s_wA[slot], amt=16)
            cko = None
            if kind == "a":
                cko = B.emit("pool", lambda g, h=h, slot=slot: g.dma_start(
                    wo_b[slot], d_wo_a[h * 128:(h + 1) * 128, :]),
                    waits=[o_war], inc=s_wO[slot], amt=16)
            return ckw, cko

        def issue_wo_pair(h0, war):
            B.emit("pool", lambda g, h0=h0: g.dma_start(
                wo_b[0], d_wo_b[h0 * 128:(h0 + 1) * 128, :]), waits=[war],
                inc=s_wO[0], amt=16)
            o_cks[0] = B.ck(s_wO[0])
            B.emit("pool", lambda g, h0=h0: g.dma_start(
                wo_b[1], d_wo_b[(h0 + 1) * 128:(h0 + 2) * 128, :]),
                inc=s_wO[1], amt=16)
            o_cks[1] = B.ck(s_wO[1])

        def ktiles(j):
            if kind == "a":
                return [(kt, None) for kt in range(j + 1)]
            if j == 0:
                return [(0, 1)]
            if j == 1:
                return [(0, 4), (1, 1)]
            return [(0, 3), (j - 1, 2), (j, 1)]

        w_cks = [None, None]
        o_cks = [None, None]
        qkv_consumed = [None, None]
        wo_consumed = [None, None]
        w_cks[0], _cko = issue_head_dma(0, 0, None, None)
        if _cko is not None:
            o_cks[0] = _cko

        probs_war = [None, None]
        probsT_war = [None, None]
        ohT_war = [None, None]
        dscr_war = None
        head_guard = None
        ck_ones = B.emit("dve", lambda v: v.tensor_scalar(
            ones_row, ident[0:1, 0:128], 0.0, 1.0, ALU.mult, ALU.add),
            waits=[ck_consts], inc=sDVE)
        if kind == "b":
            issue_wo_pair(0, None)
            ohT_cks = [[None] * NT, [None] * NT]
        for h in range(DEV_HEADS):
            slot = h % 2
            if h + 1 < DEV_HEADS:
                ns = (h + 1) % 2
                w_cks[ns], _cko = issue_head_dma(
                    h + 1, ns, qkv_consumed[ns], wo_consumed[ns])
                if _cko is not None:
                    o_cks[ns] = _cko

            if kind == "a":
                wv4 = wqkv_b[slot].rearrange("p (c s w) -> p c s w", c=ND, s=3)
                qk_specs = [(0, q_h, h), (1, k_h, 16 + h)]
            else:
                qk_specs = [(None, q_h, None), (None, k_h, None)]
            for si, (sel, dst, bc) in enumerate(qk_specs):
                for tb in range(4):
                    bank = fpool.acquire()
                    if kind == "a":
                        pairs = [(lambda c=c, sel=sel, wv4=wv4: wv4[:, c, sel, :],
                                  lambda c=c, tb=tb: xn_chunk(c)[:, tb * 512:(tb + 1) * 512])
                                 for c in range(ND)]
                        ew = [w_cks[slot], ck_xn, ck_consts]
                    elif si == 0:
                        pairs = [(lambda c=c, slot=slot: wq_b[slot][:, c * 128:(c + 1) * 128],
                                  lambda c=c, tb=tb: xn_chunk(c)[:, tb * 512:(tb + 1) * 512])
                                 for c in range(ND)]
                        ew = [w_cks[slot], ck_xn]
                    else:
                        pairs = [(lambda lc=lc, slot=slot: wk_b[slot][:, lc * 128:(lc + 1) * 128],
                                  lambda lc=lc, tb=tb: lat[:, lc * T + tb * 512:
                                                           lc * T + (tb + 1) * 512])
                                 for lc in range(4)]
                        ew = [w_cks[slot], ck_lat]
                    gend = mm_group(bank, pairs, extra_waits=ew)
                    if bc is not None:
                        eck = B.emit("act", lambda a, bank=bank, dst=dst, tb=tb,
                                     bc=bc: a.activation(
                            dst[:, tb * 512:(tb + 1) * 512], bank.ap[:, 0:512],
                            AF.Identity, bias=bqkv[:, bc:bc + 1]),
                            waits=[gend], inc=sACT)
                    else:
                        eck = B.emit("act", lambda a, bank=bank, dst=dst, tb=tb:
                                     a.activation(
                            dst[:, tb * 512:(tb + 1) * 512], bank.ap[:, 0:512],
                            AF.Identity), waits=[gend], inc=sACT)
                    bank.readers.append(eck)

            for tg in range(4):
                bank = fpool.acquire()
                fpool.pe_waits(bank)
                gend = None
                nk = ND if kind == "a" else 4
                for i in range(4):
                    tt = tg * 4 + i
                    last_c = (kind != "a")
                    for c in range(nk):
                        if kind == "a":
                            fn = (lambda p, c=c, tt=tt, i=i, bank=bank, wv4=wv4,
                                  nk=nk: p.matmul(
                                bank.ap[:, i * 128:(i + 1) * 128],
                                xn_chunk(c)[:, tt * 128:(tt + 1) * 128],
                                wv4[:, c, 2, :], start=(c == 0),
                                stop=False, skip_group_check=True))
                            ew = [w_cks[slot], ck_xn]
                        else:
                            fn = (lambda p, c=c, tt=tt, i=i, bank=bank,
                                  slot=slot, nk=nk: p.matmul(
                                bank.ap[:, i * 128:(i + 1) * 128],
                                lat[:, c * T + tt * 128:c * T + (tt + 1) * 128],
                                wv_b[slot][:, c * 128:(c + 1) * 128],
                                start=(c == 0), stop=(c == nk - 1),
                                skip_group_check=True))
                            ew = [w_cks[slot], ck_lat]
                        gend = B.emit("pe", fn,
                                      waits=ew if (c == 0 and i == 0) else [],
                                      inc=sPE if (c == nk - 1 and i == 3
                                                  and last_c) else None)
                    if kind == "a":
                        gend = B.emit("pe", lambda p, i=i, h=h, bank=bank:
                                      p.matmul(
                            bank.ap[:, i * 128:(i + 1) * 128],
                            ones_row,
                            bv_row[0:1, h * 128:(h + 1) * 128],
                            start=False, stop=True, skip_group_check=True),
                            waits=[ck_bv, ck_ones] if (tg == 0 and i == 0)
                            else [],
                            inc=sPE if i == 3 else None)
                bank.writer = gend
                eck = B.emit("act", lambda a, bank=bank, tg=tg: a.activation(
                    v_h[:, tg * 512:(tg + 1) * 512], bank.ap[:, 0:512],
                    AF.Identity), waits=[gend], inc=sACT)
                bank.readers.append(eck)
            ck_qkv = B.ck(sACT)
            if DEV_STOP == "qkv":
                continue

            out_h_cks = [None] * NT
            sums_cols = []
            recip_cks = [None] * NT
            st1 = None
            st2 = None
            st3 = None
            for j in range(NT + (2 if kind == "a" else 3)):
                nst = None
                if j < NT:
                    tj = ktiles(j)
                    tiles_j = [kt for kt, _ in tj]
                    L = len(tj) * 128
                    par = j % 2
                    pbuf = probs[par]
                    if kind == "a":
                        nchunks = ceil_div(L, 512)
                        cnt = 0
                        for kc in range(nchunks):
                            n = min(512, L - kc * 512)
                            bank = fpool.acquire()
                            gend = mm_group(bank, [(
                                lambda j=j: q_h[:, j * 128:(j + 1) * 128],
                                lambda kc=kc, n=n: k_h[:, kc * 512:kc * 512 + n])],
                                extra_waits=[ck_qkv], n_slice=n)
                            ndiag = n - 128 if kc == nchunks - 1 else n
                            eck_chunk = None
                            if ndiag > 0:
                                eck_chunk = B.emit(
                                    "act", lambda a, bank=bank, kc=kc, nd=ndiag,
                                    pbuf=pbuf, j=j, cnt=cnt: a.activation(
                                        pbuf[:, kc * 512:kc * 512 + nd],
                                        bank.ap[:, 0:nd], AF.Exp, scale=SCALE,
                                        accum_out=gsc[:, j * 5 + cnt:
                                                      j * 5 + cnt + 1]),
                                    waits=[gend, probs_war[par], head_guard],
                                    inc=sACT)
                                bank.readers.append(eck_chunk)
                                cnt += 1
                            if kc == nchunks - 1:
                                dvw = [gend, ck_consts, dscr_war]
                                if eck_chunk is not None:
                                    dvw.append(eck_chunk)
                                dck = B.emit("dve", lambda v, bank=bank, n=n:
                                             v.scalar_tensor_tensor(
                                    dscr[:, 0:128], bank.ap[:, n - 128:n],
                                    SCALE, masks[:, 0:128], ALU.mult, ALU.add),
                                    waits=dvw, inc=sDVE)
                                bank.readers.append(dck)
                                eck = B.emit("act", lambda a, pbuf=pbuf, L=L,
                                             j=j, cnt=cnt: a.activation(
                                    pbuf[:, L - 128:L], dscr[:, 0:128], AF.Exp,
                                    accum_out=gsc[:, j * 5 + cnt:
                                                  j * 5 + cnt + 1]),
                                    waits=[dck, probs_war[par], head_guard],
                                    inc=sACT)
                                dscr_war = eck
                                cnt += 1
                        sums_cols.append((j, cnt))
                    else:
                        bank = fpool.acquire()
                        fpool.pe_waits(bank)
                        gend = None
                        for i, (kt, mid) in enumerate(tj):
                            gend = B.emit("pe", lambda p, kt=kt, i=i, j=j,
                                          bank=bank: p.matmul(
                                bank.ap[:, i * 128:(i + 1) * 128],
                                q_h[:, j * 128:(j + 1) * 128],
                                k_h[:, kt * 128:(kt + 1) * 128],
                                start=True, stop=True, skip_group_check=True),
                                waits=[ck_qkv] if i == 0 else [],
                                inc=sPE if i == len(tj) - 1 else None)
                        bank.writer = gend
                        dcks = []
                        for i, (kt, mid) in enumerate(tj):
                            dck = B.emit("dve", lambda v, bank=bank, i=i,
                                         mid=mid: v.scalar_tensor_tensor(
                                dscr[:, i * 128:(i + 1) * 128],
                                bank.ap[:, i * 128:(i + 1) * 128],
                                SCALE, masks[:, mid * 128:(mid + 1) * 128],
                                ALU.mult, ALU.add),
                                waits=[gend, ck_consts, dscr_war], inc=sDVE)
                            bank.readers.append(dck)
                            dcks.append(dck)
                        eck = B.emit("act", lambda a, pbuf=pbuf, L=L, j=j:
                                     a.activation(
                            pbuf[:, 0:L], dscr[:, 0:L], AF.Exp,
                            accum_out=gsc[:, j:j + 1]),
                            waits=dcks + [probs_war[par], head_guard], inc=sACT)
                        dscr_war = eck
                        recip_cks[j] = B.emit("dve", lambda v, j=j:
                                              v.reciprocal(
                            stats[:, j:j + 1], gsc[:, j:j + 1]),
                            waits=[eck], inc=sDVE)
                    nst = (j, tiles_j, L, eck)

                nst2 = None
                if st1 is not None:
                    pj, ptiles, pL, pev = st1
                    ppar = pj % 2
                    pbuf2 = probs[ppar]
                    ptb = probsT[ppar]
                    ntj = len(ptiles)
                    ev_cks = []
                    for g0 in range(0, ntj, 4):
                        gn = min(4, ntj - g0)
                        bank2 = bpool.acquire()
                        bpool.pe_waits(bank2)
                        gend2 = None
                        for i in range(gn):
                            gend2 = B.emit("pe", lambda p, g0=g0, i=i,
                                           bank2=bank2, pbuf2=pbuf2:
                                           p.transpose(
                                bank2.ap[:, i * 128:(i + 1) * 128],
                                pbuf2[:, (g0 + i) * 128:(g0 + i + 1) * 128],
                                ident),
                                waits=[pev] if (g0 == 0 and i == 0) else [],
                                inc=sPE if i == gn - 1 else None)
                        bank2.writer = gend2
                        ev = B.emit("dve", lambda v, bank2=bank2, g0=g0, gn=gn,
                                    ptb=ptb: v.tensor_scalar(
                            ptb[:, g0 * 128:(g0 + gn) * 128],
                            bank2.ap[:, 0:gn * 128], 1.0, None, ALU.mult),
                            waits=[gend2, probsT_war[ppar]], inc=sDVE)
                        bank2.readers.append(ev)
                        ev_cks.append(ev)
                    probs_war[ppar] = B.ck(sPE)
                    nst2 = (pj, ptiles, ev_cks)

                nst3 = None
                if st2 is not None:
                    ppj, pptiles, pev_cks = st2
                    apar = ppj % 2
                    ptb2 = probsT[apar]
                    bank3 = fpool.acquire()
                    if kind == "a":
                        pairs = [(lambda kt=kt: v_h[:, kt * 128:(kt + 1) * 128],
                                  lambda i=i, ptb2=ptb2: ptb2[:, i * 128:(i + 1) * 128])
                                 for i, kt in enumerate(pptiles)]
                    else:
                        pairs = [(lambda i=i, ptb2=ptb2: ptb2[:, i * 128:(i + 1) * 128],
                                  lambda kt=kt: v_h[:, kt * 128:(kt + 1) * 128])
                                 for i, kt in enumerate(pptiles)]
                    gend3 = mm_group(bank3, pairs, extra_waits=pev_cks,
                                     n_slice=128)
                    if kind == "a":
                        oek = B.emit("act", lambda a, bank3=bank3, ppj=ppj:
                                     a.activation(
                            out_h[:, ppj * 128:(ppj + 1) * 128],
                            bank3.ap[:, 0:128], AF.Identity),
                            waits=[gend3], inc=sACT)
                    else:
                        oek = B.emit("act", lambda a, bank3=bank3, ppj=ppj:
                                     a.activation(
                            out_h[:, ppj * 128:(ppj + 1) * 128],
                            bank3.ap[:, 0:128], AF.Identity,
                            scale=stats[:, ppj:ppj + 1]),
                            waits=[gend3, recip_cks[ppj]], inc=sACT)
                        nst3 = (ppj, oek)
                    bank3.readers.append(oek)
                    out_h_cks[ppj] = oek
                    probsT_war[apar] = B.ck(sPE)

                if kind == "b" and st3 is not None:
                    qj, qck = st3
                    bank4 = bpool.acquire()
                    bpool.pe_waits(bank4)
                    gend4 = B.emit("pe", lambda p, qj=qj, bank4=bank4:
                                   p.transpose(
                        bank4.ap[:, 0:128],
                        out_h[:, qj * 128:(qj + 1) * 128], ident),
                        waits=[qck], inc=sPE)
                    bank4.writer = gend4
                    ohe = B.emit("act", lambda a, qj=qj, bank4=bank4, hp=h % 2:
                                 a.activation(
                        ohT[hp][:, qj * 128:(qj + 1) * 128],
                        bank4.ap[:, 0:128], AF.Identity),
                        waits=[gend4, ohT_war[h % 2]], inc=sACT)
                    bank4.readers.append(ohe)
                    ohT_cks[h % 2][qj] = ohe
                st3 = nst3
                st2 = nst2
                st1 = nst

            qkv_consumed[slot] = B.ck(sPE)

            if kind == "a":
                ck_allact = B.ck(sACT)
                for (j2, cnt2) in sums_cols:
                    B.emit("dve", lambda v, j2=j2, cnt2=cnt2: v.reduce_sum(
                        stats[:, 16 + j2:17 + j2],
                        gsc[:, j2 * 5:j2 * 5 + cnt2], AX), waits=[ck_allact])
                B.drain("dve")
                head_guard = B.emit("dve", lambda v: v.reciprocal(
                    stats[:, 0:16], stats[:, 16:32]),
                    waits=[ck_allact], inc=sDVE)
                B.drain("dve")

                for tt in range(NT):
                    last = None
                    for od in range(4):
                        bank = fpool.acquire()
                        gend = mm_group(bank, [(
                            lambda tt=tt: out_h[:, tt * 128:(tt + 1) * 128],
                            lambda od=od, slot=slot: wo_b[slot][:, od * 512:(od + 1) * 512])],
                            extra_waits=[out_h_cks[tt], o_cks[slot]])
                        last = B.emit("dve", lambda v, bank=bank, tt=tt, od=od:
                                      v.scalar_tensor_tensor(
                            res_slice(tt, od * 512, (od + 1) * 512),
                            bank.ap[:, 0:512], stats[:, tt:tt + 1],
                            res_slice(tt, od * 512, (od + 1) * 512),
                            ALU.mult, ALU.add),
                            waits=[gend], inc=sDVE)
                        bank.readers.append(last)
                    res_ready[tt] = last
                wo_consumed[slot] = B.ck(sPE)
            else:
                head_guard = B.ck(sDVE)
                if h % 2 == 1:
                    for tt in range(NT):
                        last = None
                        for od in range(4):
                            bank = fpool.acquire()
                            pairs = [(lambda tt=tt, pp=pp: ohT[pp][:, tt * 128:(tt + 1) * 128],
                                      lambda od=od, pp=pp: wo_b[pp][:, od * 512:(od + 1) * 512])
                                     for pp in range(2)]
                            gend = mm_group(bank, pairs, extra_waits=[
                                ohT_cks[0][tt], ohT_cks[1][tt],
                                o_cks[0], o_cks[1]])
                            last = B.emit("dve", lambda v, bank=bank, tt=tt,
                                          od=od: v.tensor_tensor(
                                res_slice(tt, od * 512, (od + 1) * 512),
                                bank.ap[:, 0:512],
                                res_slice(tt, od * 512, (od + 1) * 512),
                                ALU.add), waits=[gend], inc=sDVE)
                            bank.readers.append(last)
                        res_ready[tt] = last
                    wo_consumed[0] = wo_consumed[1] = B.ck(sPE)
                    ohT_war[0] = ohT_war[1] = B.ck(sPE)
                    if h + 2 < DEV_HEADS:
                        issue_wo_pair(h + 1, B.ck(sPE))

        if kind == "a":
            ck_bo = B.emit("pool", lambda g: g.dma_start(bo_bc, d_bo[:, :]),
                           waits=[xn_guard[0]], inc=s_bo, amt=16)
            for tt in range(NT):
                res_ready[tt] = B.emit("dve", lambda v, tt=tt: v.tensor_tensor(
                    res_tile(tt), res_tile(tt), bo_bc, ALU.add),
                    waits=[ck_bo, res_ready[tt]], inc=sDVE)

        fence = phase_fence()
        ph.close()
        return res_ready, fence

    moe_ctr = [0]

    def moe_phase(res_ready, final):
        ph = ExitStack()
        sfx = str(moe_ctr[0])
        moe_ctr[0] += 1

        def psb(name, shape, dt):
            return ph.enter_context(nc.sbuf_tensor(name + sfx, shape, dt))[:]

        TH = T // 2
        xn = psb("xnm", [128, ND * TH], BF16)
        xn_tm = [psb(f"xntmm{i}", [128, DM], BF16) for i in range(2)]
        Hbuf = psb("Hm", [128, NI * TH], BF16)
        w1b = [psb(f"w1b{i}", [128, ND * 128], BF16) for i in range(2)]
        w3b = [psb(f"w3b{i}", [128, ND * 128], BF16) for i in range(2)]
        w2b = [psb(f"w2b{i}", [128, NI * 512], BF16) for i in range(2)]
        sA = [psb(f"sAb{i}", [128, 512], BF16) for i in range(2)]
        ostage = [psb(f"ost{i}", [128, DM], F32) for i in range(2)] \
            if final else [None, None]

        def xn_chunk(c):
            return xn[:, c * TH:(c + 1) * TH]

        xnv = xn.rearrange("p (c w) -> p c w", c=ND)
        ck_r, ck_negmr = ln_stats(res_ready)

        def wsrc(u):
            if u < 2:
                return (lambda ic, u=u: d_sw1[u, ic],
                        lambda ic, u=u: d_sw3[u, ic],
                        lambda dq, u=u: d_sw2[u, dq], None)
            e = u - 2
            return (lambda ic, e=e: d_w1[e, ic],
                    lambda ic, e=e: d_w3[e, ic],
                    lambda dq, e=e: d_w2[e, dq], e)

        w13_consumed = [None, None]
        w2_consumed = [None, None]
        sA_war = [None, None]
        w13_ready = {}
        w13_slot = {}
        w2_ready = {}
        w2_slot = {}
        ctr = {"w13": 0, "w2": 0}

        def issue_w13(half, u, ic):
            slot = ctr["w13"] % 2
            ctr["w13"] += 1
            s1, s3, _, _ = wsrc(u)
            B.emit("pool", lambda g, s1=s1, ic=ic, slot=slot: g.dma_start(
                w1b[slot], s1(ic)), waits=[w13_consumed[slot]],
                inc=s_w13[slot], amt=16)
            ck = B.emit("pool", lambda g, s3=s3, ic=ic, slot=slot: g.dma_start(
                w3b[slot], s3(ic)), inc=s_w13[slot], amt=16)
            w13_ready[(half, u, ic)] = ck
            w13_slot[(half, u, ic)] = slot

        def issue_w2(half, u, dq):
            slot = ctr["w2"] % 2
            ctr["w2"] += 1
            _, _, s2, _ = wsrc(u)
            ck = B.emit("pool", lambda g, s2=s2, dq=dq, slot=slot: g.dma_start(
                w2b[slot], s2(dq)), waits=[w2_consumed[slot]],
                inc=s_w2[slot], amt=16)
            w2_ready[(half, u, dq)] = ck
            w2_slot[(half, u, dq)] = slot

        res_done = list(res_ready)
        out_cks = []
        ostage_free = [None, None]
        dump_xn_ck = [None]
        dump_h_pending = [None]
        for half in range(2):
            t0 = half * 8
            if half == 1 and dump_xn_ck[0] is not None:
                B.wait("act", dump_xn_ck[0])
            xn_guard = [None, None]
            for tl in range(8):
                t = t0 + tl
                buf = tl % 2
                ck_tm = make_xn_tile(xn_tm[buf], t, ck_r, ck_negmr,
                                     res_done[t], xn_guard[buf])
                transpose_to_fm(
                    xn_tm[buf], ck_tm,
                    lambda c0, tl=tl: xnv[:, c0:c0 + 4, tl * 128:(tl + 1) * 128])
                xn_guard[buf] = B.ck(sPE)
            ck_xn = B.ck(sACT)
            if DEV_MOE_DUMP and half == 0:
                dump_xn_ck[0] = B.emit("sync", lambda s: s.dma_start(
                    d_mxn[:, :], xn), waits=[ck_xn], inc=sS, amt=16)

            for tl in range(8):
                t = t0 + tl
                bank = fpool.acquire()
                pairs = [(lambda c=c, tl=tl: xn_chunk(c)[:, tl * 128:(tl + 1) * 128],
                          lambda c=c: gw[:, c * 16:(c + 1) * 16])
                         for c in range(ND)]
                gend = mm_group(bank, pairs, extra_waits=[ck_xn, ck_gw],
                                n_slice=16)
                ck_gl = B.emit("act", lambda a, bank=bank: a.activation(
                    gsc[:, 0:16], bank.ap[:, 0:16], AF.Identity),
                    waits=[gend], inc=sACT)
                bank.readers.append(ck_gl)
                ck_gm = B.emit("dve", lambda v: v.reduce_max(
                    tiny[:, 4:5], gsc[:, 0:16], AX, negate=True),
                    waits=[ck_gl], inc=sDVE)
                ck_ge = B.emit("act", lambda a: a.activation(
                    gsc[:, 16:32], gsc[:, 0:16], AF.Exp,
                    bias=tiny[:, 4:5], accum_out=tiny[:, 5:6]),
                    waits=[ck_gm], inc=sACT)
                B.emit("dve", lambda v: v.reciprocal(tiny[:, 6:7], tiny[:, 5:6]),
                       waits=[ck_ge])
                B.drain("dve")
                B.emit("dve", lambda v: v.tensor_scalar(
                    gsc[:, 32:48], gsc[:, 16:32], tiny[:, 6:7], None, ALU.mult))
                B.drain("dve")
                B.emit("dve", lambda v: v.reduce_max(
                    tiny[:, 4:5], gsc[:, 32:48], AX))
                B.drain("dve")
                B.emit("dve", lambda v: v.tensor_scalar(
                    gsc[:, 48:64], gsc[:, 32:48], tiny[:, 4:5], None, ALU.is_lt))
                B.drain("dve")
                B.emit("dve", lambda v: v.tensor_tensor(
                    gsc[:, 64:80], gsc[:, 32:48], gsc[:, 48:64], ALU.mult))
                B.drain("dve")
                B.emit("dve", lambda v: v.reduce_max(
                    tiny[:, 5:6], gsc[:, 64:80], AX))
                B.drain("dve")
                B.emit("dve", lambda v: v.tensor_scalar(
                    gsc[:, 48:64], gsc[:, 32:48], tiny[:, 5:6], None, ALU.is_ge))
                B.drain("dve")
                B.emit("dve", lambda v, t=t: v.tensor_tensor(
                    wgate[:, t * 16:(t + 1) * 16], gsc[:, 32:48],
                    gsc[:, 48:64], ALU.mult), inc=sDVE)
                B.drain("dve")
            ck_gate = B.ck(sDVE)
            if DEV_MOE_DUMP and half == 0:
                B.emit("sync", lambda s: s.dma_start(d_mwg[:, :],
                                                     wgate[:, 0:256]),
                       waits=[ck_gate], inc=sS, amt=16)

            issue_w13(half, 0, 0)
            issue_w2(half, 0, 0)

            for u in range(NU):
                _, _, _, e = wsrc(u)
                if dump_h_pending[0] is not None:
                    B.wait("dve", dump_h_pending[0])
                    dump_h_pending[0] = None
                for ic in range(NI):
                    if ic + 1 < NI:
                        issue_w13(half, u, ic + 1)
                    elif u + 1 < NU:
                        issue_w13(half, u + 1, 0)
                    elif half == 0:
                        issue_w13(1, 0, 0)
                    slot = w13_slot[(half, u, ic)]
                    wck = w13_ready[(half, u, ic)]
                    for tb in range(2):
                        bankA = fpool.acquire()
                        pairs = [(lambda c=c, slot=slot: w1b[slot][:, c * 128:(c + 1) * 128],
                                  lambda c=c, tb=tb: xn_chunk(c)[:, tb * 512:(tb + 1) * 512])
                                 for c in range(ND)]
                        ckA = mm_group(bankA, pairs, extra_waits=[wck, ck_xn])
                        bankB = fpool.acquire()
                        pairs = [(lambda c=c, slot=slot: w3b[slot][:, c * 128:(c + 1) * 128],
                                  lambda c=c, tb=tb: xn_chunk(c)[:, tb * 512:(tb + 1) * 512])
                                 for c in range(ND)]
                        ckB = mm_group(bankB, pairs)
                        sbuf = sA[tb]
                        ck_s = B.emit("act", lambda a, bankA=bankA, sbuf=sbuf:
                                      a.activation(sbuf, bankA.ap[:, 0:512],
                                                   AF.Silu),
                                      waits=[ckA, sA_war[tb]], inc=sACT)
                        bankA.readers.append(ck_s)
                        ck_h = B.emit("dve", lambda v, bankB=bankB, sbuf=sbuf,
                                      ic=ic, tb=tb: v.tensor_tensor(
                            Hbuf[:, ic * TH + tb * 512:ic * TH + (tb + 1) * 512],
                            sbuf, bankB.ap[:, 0:512], ALU.mult),
                            waits=[ckB, ck_s], inc=sDVE)
                        bankB.readers.append(ck_h)
                        sA_war[tb] = ck_h
                    w13_consumed[slot] = B.ck(sPE)
                ck_H = B.ck(sDVE)
                if DEV_MOE_DUMP and half == 0 and u in (0, 2):
                    dump_h_pending[0] = B.emit(
                        "sync", lambda s, u=u: s.dma_start(
                            d_mh[u][:, :], Hbuf), waits=[ck_H], inc=sS, amt=16)

                for dq in range(4):
                    if dq + 1 < 4:
                        issue_w2(half, u, dq + 1)
                    elif u + 1 < NU:
                        issue_w2(half, u + 1, 0)
                    elif half == 0:
                        issue_w2(1, 0, 0)
                    slot = w2_slot[(half, u, dq)]
                    wck = w2_ready[(half, u, dq)]
                    for tl in range(8):
                        t = t0 + tl
                        bank = fpool.acquire()
                        pairs = [(lambda ic=ic, tl=tl: Hbuf[:, ic * TH + tl * 128:
                                                            ic * TH + (tl + 1) * 128],
                                  lambda ic=ic, slot=slot: w2b[slot][:, ic * 512:
                                                                     (ic + 1) * 512])
                                 for ic in range(NI)]
                        gend = mm_group(bank, pairs,
                                        extra_waits=[wck, ck_H, ck_gate])
                        if e is None:
                            eck = B.emit("dve", lambda v, bank=bank, t=t, dq=dq:
                                         v.tensor_tensor(
                                res_slice(t, dq * 512, (dq + 1) * 512),
                                bank.ap[:, 0:512],
                                res_slice(t, dq * 512, (dq + 1) * 512), ALU.add),
                                waits=[gend], inc=sDVE)
                        else:
                            eck = B.emit("dve", lambda v, bank=bank, t=t, dq=dq,
                                         e=e: v.scalar_tensor_tensor(
                                res_slice(t, dq * 512, (dq + 1) * 512),
                                bank.ap[:, 0:512],
                                wgate[:, t * 16 + e:t * 16 + e + 1],
                                res_slice(t, dq * 512, (dq + 1) * 512),
                                ALU.mult, ALU.add),
                                waits=[gend], inc=sDVE)
                        bank.readers.append(eck)
                        if u == NU - 1:
                            res_done[t] = eck
                            if final and dq == 3:
                                obuf = ostage[tl % 2]
                                cko = B.emit("act", lambda a, obuf=obuf, t=t:
                                             a.activation(obuf, res_tile(t),
                                                          AF.Identity),
                                             waits=[eck, ostage_free[tl % 2]],
                                             inc=sACT)
                                dck = B.emit("sync", lambda s, obuf=obuf, t=t:
                                             s.dma_start(
                                    dy[t * 128:(t + 1) * 128, :], obuf),
                                    waits=[cko], inc=sOUT[tl % 2], amt=16)
                                out_cks.append(dck)
                                ostage_free[tl % 2] = dck
                    w2_consumed[slot] = B.ck(sPE)

        fence = phase_fence()
        ph.close()
        return res_done, fence, out_cks

    cur = res_ready
    out_cks = []
    fence = None
    for i, sl in enumerate(sublayers):
        if fence is not None:
            apply_fence(fence)
        final = (i == len(sublayers) - 1)
        if sl.startswith("attn"):
            cur, fence = attn_phase("a" if sl == "attn_a" else "b", cur)
            if final:
                obuf = es.enter_context(
                    nc.sbuf_tensor("odbg", [128, DM], F32))[:]
                prev = None
                for t in range(NT):
                    cko = B.emit("act", lambda a, t=t, obuf=obuf: a.activation(
                        obuf, res_tile(t), AF.Identity),
                        waits=[cur[t], prev], inc=sACT)
                    prev = B.emit("sync", lambda s, t=t, obuf=obuf: s.dma_start(
                        dy[t * 128:(t + 1) * 128, :], obuf),
                        waits=[cko], inc=sOUT[t % 2], amt=16)
                    out_cks.append(prev)
        else:
            cur, fence, ocks = moe_phase(cur, final)
            out_cks.extend(ocks)
            if final and not ocks:
                raise RuntimeError("final moe produced no outputs")

    B.wait("sync", (sOUT[0], sOUT[0].n))
    B.wait("sync", (sOUT[1], sOUT[1].n))

    with nc.Block() as block:
        @block.tensor
        def _(eng):
            for f in B.ops["pe"]:
                f(eng)

        @block.scalar
        def _(eng):
            for f in B.ops["act"]:
                f(eng)

        @block.vector
        def _(eng):
            for f in B.ops["dve"]:
                f(eng)

        @block.gpsimd
        def _(eng):
            for f in B.ops["pool"]:
                f(eng)

        @block.sync
        def _(eng):
            for f in B.ops["sync"]:
                f(eng)

    es.close()
    return nc


def _make_masks():
    q = np.arange(128)[:, None]
    k = np.arange(128)[None, :]
    causal_a = np.where(k <= q, 0.0, -10000.0)
    causal_b = np.where(k <= q, 0.0, -60000.0)
    upper_b = np.where(k > q, 0.0, -60000.0)
    persist_b = np.where(np.broadcast_to(k < 16, (128, 128)), 0.0, -60000.0)
    union_b = np.where((k > q) | (k < 16), 0.0, -60000.0)
    m = np.concatenate([causal_a, causal_b, upper_b, persist_b, union_b],
                       axis=1).astype(ml_dtypes.bfloat16)
    return np.ascontiguousarray(m)


def _prep_shared(Wqkv, bqkv, Wo_a, bo, Wq, Wkv, Wk, Wv, Wo_b,
                 gate_w, w1, w2, w3, sw1, sw2, sw3):
    f32 = np.float32
    c = np.ascontiguousarray
    Wqkv = np.asarray(Wqkv, f32)
    wqkvh = c(Wqkv.reshape(16, 128, 3, 16, 128).transpose(3, 1, 0, 2, 4)
              .reshape(NH, 128, ND * 384))
    wq = np.asarray(Wq, f32)
    wqh = c(wq.reshape(16, 128, 16, 128).transpose(2, 1, 0, 3)
            .reshape(NH, 128, ND * 128))
    wk = np.asarray(Wk, f32)
    wkh = c(wk.reshape(4, 128, 16, 128).transpose(2, 1, 0, 3)
            .reshape(NH, 128, 4 * 128))
    wv = np.asarray(Wv, f32)
    wvh = c(wv.reshape(4, 128, 16, 128).transpose(2, 1, 0, 3)
            .reshape(NH, 128, 4 * 128))
    wkvb = c(np.asarray(Wkv, f32).reshape(16, 128, 512).transpose(1, 0, 2)
             .reshape(128, ND * 512))
    gwc = c(np.asarray(gate_w, f32).T.reshape(16, 128, 16).transpose(1, 0, 2)
            .reshape(128, ND * 16))
    w1 = np.asarray(w1, f32)
    w1c = c(w1.transpose(0, 2, 1).reshape(16, 16, 128, 11, 128)
            .transpose(0, 3, 2, 1, 4).reshape(16, NI, 128, ND * 128))
    w3 = np.asarray(w3, f32)
    w3c = c(w3.transpose(0, 2, 1).reshape(16, 16, 128, 11, 128)
            .transpose(0, 3, 2, 1, 4).reshape(16, NI, 128, ND * 128))
    w2 = np.asarray(w2, f32)
    w2c = c(w2.transpose(0, 2, 1).reshape(16, 11, 128, 4, 512)
            .transpose(0, 3, 2, 1, 4).reshape(16, 4, 128, NI * 512))
    sw1 = np.asarray(sw1, f32)
    sw1c = c(sw1.T.reshape(16, 128, 2, 11, 128).transpose(2, 3, 1, 0, 4)
             .reshape(2, NI, 128, ND * 128))
    sw3 = np.asarray(sw3, f32)
    sw3c = c(sw3.T.reshape(16, 128, 2, 11, 128).transpose(2, 3, 1, 0, 4)
             .reshape(2, NI, 128, ND * 128))
    sw2 = np.asarray(sw2, f32)
    sw2c = c(sw2.T.reshape(2, 11, 128, 4, 512).transpose(0, 3, 2, 1, 4)
             .reshape(2, 4, 128, NI * 512))
    return {
        "wqkvh": wqkvh,
        "bqkvt": c(np.asarray(bqkv, f32).reshape(48, 128).T),
        "bvrow": c(np.asarray(bqkv, f32)[2 * DM:3 * DM].reshape(1, DM)),
        "wo_a": c(np.asarray(Wo_a, f32)),
        "bo_bc": c(np.broadcast_to(np.asarray(bo, f32)[None, :], (128, DM))),
        "wqh": wqh, "wkvb": wkvb, "wkh": wkh, "wvh": wvh,
        "wo_b": c(np.asarray(Wo_b, f32)),
        "gwc": gwc, "w1c": w1c, "w3c": w3c, "w2c": w2c,
        "sw1c": sw1c, "sw3c": sw3c, "sw2c": sw2c,
        "ident": np.eye(128, dtype=ml_dtypes.bfloat16),
        "masks": _make_masks(),
    }


_PROG_CACHE = {}


def _get_prog(subl=SUBLAYERS):
    key = tuple(subl)
    if key not in _PROG_CACHE:
        _PROG_CACHE[key] = build_program(subl)
    return _PROG_CACHE[key]


def kernel(L0_attn_a_x, L0_attn_a_Wqkv, L0_attn_a_bqkv, L0_attn_a_Wo,
           L0_attn_a_bo, L0_attn_b_Wq, L0_attn_b_Wkv, L0_attn_b_Wk,
           L0_attn_b_Wv, L0_attn_b_Wo, L0_ffn_gate_w, L0_ffn_w1, L0_ffn_w2,
           L0_ffn_w3, L0_ffn_sw1, L0_ffn_sw2, L0_ffn_sw3, _subl=SUBLAYERS):
    shared = _prep_shared(L0_attn_a_Wqkv, L0_attn_a_bqkv, L0_attn_a_Wo,
                          L0_attn_a_bo, L0_attn_b_Wq, L0_attn_b_Wkv,
                          L0_attn_b_Wk, L0_attn_b_Wv, L0_attn_b_Wo,
                          L0_ffn_gate_w, L0_ffn_w1, L0_ffn_w2, L0_ffn_w3,
                          L0_ffn_sw1, L0_ffn_sw2, L0_ffn_sw3)
    x = np.asarray(L0_attn_a_x, np.float32)
    nc = _get_prog(_subl)
    in_maps = []
    for i in range(8):
        m = dict(shared)
        m["x"] = np.ascontiguousarray(x[i])
        in_maps.append(m)
    out = run_bass_kernel_spmd(nc, in_maps, core_ids=list(range(8)))
    global LAST_RESULTS
    LAST_RESULTS = out.results
    return np.stack([r["y"] for r in out.results], axis=0)


LAST_RESULTS = None


# version 07
# speedup vs baseline: 1.1044x; 1.1044x over previous
import numpy as np
import ml_dtypes
from contextlib import ExitStack

import concourse.bass as bass
import concourse.mybir as mybir
from concourse.bass_utils import run_bass_kernel_spmd

F32 = mybir.dt.float32
BF16 = mybir.dt.bfloat16
AF = mybir.ActivationFunctionType
ALU = mybir.AluOpType
AX = mybir.AxisListType.X

T = 2048
DM = 2048
NT = 16
ND = 16
NH = 16
HD = 128
INTER = 1408
NI = 11
NU = 18
SCALE = 1.0 / (HD ** 0.5)
EPS = 1e-5

SUBLAYERS = ("attn_a", "moe", "attn_b", "moe")
DEV_DEBUG = False
DEV_MOE_DUMP = False
DEV_HEADS = NH
DEV_STOP = None
ENGS = ("pe", "act", "dve", "pool", "sync")


class Sem:
    def __init__(self, handle, name, owner=None):
        self.h = handle
        self.name = name
        self.owner = owner
        self.n = 0


class Builder:
    def __init__(self):
        self.ops = {k: [] for k in ENGS}
        self.waited = {k: {} for k in ENGS}

    def wait(self, eng, ck):
        if ck is None:
            return
        sem, val = ck
        if val <= 0:
            return
        assert val <= sem.n, f"wait on future: {sem.name} {val} > {sem.n}"
        if sem.owner == eng:
            return
        if self.waited[eng].get(sem, 0) >= val:
            return
        self.waited[eng][sem] = val
        self.ops[eng].append(lambda e, s=sem, v=val: e.wait_ge(s.h, v))

    def emit(self, eng, fn, waits=(), inc=None, amt=1):
        for ck in waits:
            self.wait(eng, ck)
        if inc is not None:
            inc.n += amt
            self.ops[eng].append(
                lambda e, f=fn, s=inc, a=amt: f(e).then_inc(s.h, a))
            return (inc, inc.n)
        self.ops[eng].append(fn)
        return None

    def drain(self, eng):
        self.ops[eng].append(lambda e: e.drain())

    def ck(self, sem):
        return (sem, sem.n)


class Bank:
    def __init__(self, ap):
        self.ap = ap
        self.writer = None
        self.readers = []


class BankPool:
    def __init__(self, b, banks):
        self.b = b
        self.banks = banks
        self.i = 0

    def acquire(self):
        bank = self.banks[self.i % len(self.banks)]
        self.i += 1
        return bank

    def pe_waits(self, bank):
        for ck in bank.readers:
            self.b.wait("pe", ck)
        bank.readers = []


def ceil_div(a, b):
    return -(-a // b)


def build_program(sublayers=SUBLAYERS):
    nc = bass.Bass("TRN2", target_bir_lowering=False,
                   detect_race_conditions=False)

    dx = nc.dram_tensor("x", [T, DM], F32, kind="ExternalInput")
    dy = nc.dram_tensor("y", [T, DM], F32, kind="ExternalOutput")
    d_wqkv = nc.dram_tensor("wqkvh", [NH, 128, ND * 384], F32, kind="ExternalInput")
    d_bqkv = nc.dram_tensor("bqkvt", [128, 48], F32, kind="ExternalInput")
    d_bvrow = nc.dram_tensor("bvrow", [1, DM], F32, kind="ExternalInput")
    d_wo_a = nc.dram_tensor("wo_a", [DM, DM], F32, kind="ExternalInput")
    d_bo = nc.dram_tensor("bo_bc", [128, DM], F32, kind="ExternalInput")
    d_wq_b = nc.dram_tensor("wqh", [NH, 128, ND * 128], F32, kind="ExternalInput")
    d_wkv_b = nc.dram_tensor("wkvb", [128, ND * 512], F32, kind="ExternalInput")
    d_wk_b = nc.dram_tensor("wkh", [NH, 128, 4 * 128], F32, kind="ExternalInput")
    d_wv_b = nc.dram_tensor("wvh", [NH, 128, 4 * 128], F32, kind="ExternalInput")
    d_wo_b = nc.dram_tensor("wo_b", [DM, DM], F32, kind="ExternalInput")
    d_gw = nc.dram_tensor("gwc", [128, ND * 16], F32, kind="ExternalInput")
    d_w1 = nc.dram_tensor("w1c", [16, NI, 128, ND * 128], F32, kind="ExternalInput")
    d_w3 = nc.dram_tensor("w3c", [16, NI, 128, ND * 128], F32, kind="ExternalInput")
    d_w2 = nc.dram_tensor("w2c", [16, 4, 128, NI * 512], F32, kind="ExternalInput")
    d_sw1 = nc.dram_tensor("sw1c", [2, NI, 128, ND * 128], F32, kind="ExternalInput")
    d_sw3 = nc.dram_tensor("sw3c", [2, NI, 128, ND * 128], F32, kind="ExternalInput")
    d_sw2 = nc.dram_tensor("sw2c", [2, 4, 128, NI * 512], F32, kind="ExternalInput")
    d_ident = nc.dram_tensor("ident", [128, 128], BF16, kind="ExternalInput")
    if DEV_DEBUG:
        d_dbg = {nm: nc.dram_tensor(f"dbg_{nm}", [128, T], BF16,
                                    kind="ExternalOutput")
                 for nm in ("qh", "kh", "vh", "oh", "probs", "probsT", "scores")}
    d_masks = nc.dram_tensor("masks", [128, 5 * 128], BF16, kind="ExternalInput")
    if DEV_MOE_DUMP:
        d_mxn = nc.dram_tensor("mdump_xn", [128, ND * 1024], BF16,
                               kind="ExternalOutput")
        d_mwg = nc.dram_tensor("mdump_wg", [128, 16 * 16], F32,
                               kind="ExternalOutput")
        d_mh = {u: nc.dram_tensor(f"mdump_h{u}", [128, NI * 1024], BF16,
                                  kind="ExternalOutput") for u in (0, 2)}

    es = ExitStack()

    def sb(name, shape, dt):
        return es.enter_context(nc.sbuf_tensor(name, shape, dt))[:]

    res = sb("res", [128, NT * DM], BF16)
    ident = sb("identsb", [128, 128], BF16)
    masks = sb("maskssb", [128, 5 * 128], BF16)
    bqkv = sb("bqkvsb", [128, 48], F32)
    gw = sb("gwsb", [128, ND * 16], BF16)
    wgate = sb("wgatesb", [128, NT * 16], F32)
    stats = sb("statssb", [128, 64], F32)
    gsc = sb("gscsb", [128, 96], F32)
    tiny = sb("tinysb", [128, 8], F32)
    sq_scr = sb("sqscr", [128, DM], BF16)

    fbanks = [Bank(es.enter_context(
        nc.psum_tensor(f"pf{i}", [128, 512], F32))[:]) for i in range(6)]
    bbanks = [Bank(es.enter_context(
        nc.psum_tensor(f"pb{i}", [128, 1024], BF16))[:]) for i in range(2)]

    sems = []

    def mksem(name, owner):
        h = es.enter_context(nc.semaphore(name))
        s = Sem(h, name, owner)
        sems.append(s)
        return s

    sPE = mksem("sPE", "pe")
    sACT = mksem("sACT", "act")
    sDVE = mksem("sDVE", "dve")
    sX = mksem("sX", None)
    sS = mksem("sS", None)
    s_bo = mksem("sBO", None)
    s_wA = [mksem("sWA0", None), mksem("sWA1", None)]
    s_wO = [mksem("sWO0", None), mksem("sWO1", None)]
    s_w13 = [mksem("sW13a", None), mksem("sW13b", None), mksem("sW13c", None)]
    s_w2 = [mksem("sW2a", None), mksem("sW2b", None)]
    sOUT = [mksem("sOUT0", None), mksem("sOUT1", None)]

    with nc.Block() as b0:
        @b0.gpsimd
        def _(g):
            for s in sems:
                g.sem_clear(s.h)

    B = Builder()
    fpool = BankPool(B, fbanks)
    bpool = BankPool(B, bbanks)

    def res_tile(t):
        return res[:, t * DM:(t + 1) * DM]

    def res_slice(t, d0, d1):
        return res[:, t * DM + d0:t * DM + d1]

    for t in range(NT):
        B.emit("pool", lambda g, t=t: g.dma_start(
            res_tile(t), dx[t * 128:(t + 1) * 128, :]), inc=sX, amt=16)
    B.emit("sync", lambda s: s.dma_start(ident, d_ident[:, :]), inc=sS, amt=16)
    B.emit("sync", lambda s: s.dma_start(masks, d_masks[:, :]), inc=sS, amt=16)
    B.emit("sync", lambda s: s.dma_start(bqkv, d_bqkv[:, :]), inc=sS, amt=16)
    ck_gw = B.emit("pool", lambda g: g.dma_start(gw, d_gw[:, :]), inc=sX, amt=16)
    ck_x = B.ck(sX)
    res_ready = [ck_x] * NT
    ck_consts = B.ck(sS)

    def ln_stats(res_ready):
        for t in range(NT):
            B.emit("dve", lambda v, t=t: v.reduce_sum(
                stats[:, t:t + 1], res_tile(t), AX), waits=[res_ready[t]])
        B.drain("dve")
        ck_sq = None
        for t in range(NT):
            ck_sq = B.emit("act", lambda a, t=t: a.activation(
                sq_scr, res_tile(t), AF.Square,
                accum_out=stats[:, 16 + t:17 + t]),
                waits=[res_ready[t]], inc=sACT)
        B.emit("dve", lambda v: v.tensor_scalar(
            gsc[:, 0:16], stats[:, 0:16], 1.0 / DM, None, ALU.mult))
        B.drain("dve")
        B.emit("dve", lambda v: v.tensor_tensor(
            gsc[:, 16:32], gsc[:, 0:16], gsc[:, 0:16], ALU.mult))
        B.emit("dve", lambda v: v.tensor_scalar(
            gsc[:, 32:48], stats[:, 16:32], 1.0 / DM, EPS, ALU.mult, ALU.add),
            waits=[ck_sq])
        B.drain("dve")
        B.emit("dve", lambda v: v.tensor_tensor(
            gsc[:, 48:64], gsc[:, 32:48], gsc[:, 16:32], ALU.subtract))
        B.drain("dve")
        ck_rv = B.emit("dve", lambda v: v.reciprocal(
            gsc[:, 64:80], gsc[:, 48:64]), inc=sDVE)
        ck_r = B.emit("act", lambda a: a.activation(
            stats[:, 48:64], gsc[:, 64:80], AF.Sqrt), waits=[ck_rv], inc=sACT)
        ck_negmr = B.emit("dve", lambda v: v.scalar_tensor_tensor(
            stats[:, 32:48], gsc[:, 0:16], -1.0, stats[:, 48:64],
            ALU.mult, ALU.mult), waits=[ck_r], inc=sDVE)
        B.drain("dve")
        return ck_r, ck_negmr

    def make_xn_tile(dst, t, ck_r, ck_negmr, res_ck, guard):
        return B.emit("act", lambda a, dst=dst, t=t: a.activation(
            dst, res_tile(t), AF.Identity,
            bias=stats[:, 32 + t:33 + t], scale=stats[:, 48 + t:49 + t]),
            waits=[ck_r, ck_negmr, res_ck, guard], inc=sACT)

    def transpose_to_fm(src_tm, ck_tm, dst_fn, n_chunks=ND):
        last_eck = None
        for c0 in range(0, n_chunks, 4):
            bank = bpool.acquire()
            bpool.pe_waits(bank)
            gend = None
            for i in range(4):
                c = c0 + i
                gend = B.emit("pe", lambda p, c=c, i=i, bank=bank, src=src_tm:
                              p.transpose(
                    bank.ap[:, i * 128:(i + 1) * 128],
                    src[:, c * 128:(c + 1) * 128], ident),
                    waits=[ck_tm, ck_consts] if i == 0 else [],
                    inc=sPE if i == 3 else None)
            bank.writer = gend
            last_eck = B.emit("act", lambda a, bank=bank, c0=c0: a.activation(
                dst_fn(c0), bank.ap[:, 0:512], AF.Identity),
                waits=[gend], inc=sACT)
            bank.readers.append(last_eck)
        return last_eck

    def mm_group(bank, pairs, extra_waits=(), n_slice=512):
        fpool.pe_waits(bank)
        n = len(pairs)
        gend = None
        for i, (lf, rf) in enumerate(pairs):
            gend = B.emit("pe", lambda p, lf=lf, rf=rf, i=i, n=n, bank=bank,
                          ns=n_slice: p.matmul(
                bank.ap[:, 0:ns], lf(), rf(),
                start=(i == 0), stop=(i == n - 1)),
                waits=list(extra_waits) if i == 0 else [],
                inc=sPE if i == n - 1 else None)
        bank.writer = gend
        return gend

    def phase_fence():
        return [B.ck(sPE), B.ck(sACT), B.ck(sDVE)]

    def apply_fence(fence):
        for eng in ("pe", "act", "dve", "pool"):
            for f in fence:
                B.wait(eng, f)

    def attn_phase(kind, res_ready):
        ph = ExitStack()

        def psb(name, shape, dt):
            return ph.enter_context(nc.sbuf_tensor(name, shape, dt))[:]

        tag = kind
        xn = psb(f"xn{tag}", [128, ND * T], BF16)
        xn_tm = [psb(f"xntm{tag}{i}", [128, DM], BF16) for i in range(2)]
        SL = T if kind == "a" else 512
        if kind == "b":
            lat = psb("lat", [128, 4 * T], BF16)

        def xn_chunk(c):
            return xn[:, c * T:(c + 1) * T]

        xnv = xn.rearrange("p (c w) -> p c w", c=ND)

        ck_r, ck_negmr = ln_stats(res_ready)
        xn_guard = [None, None]
        for t in range(NT):
            buf = t % 2
            ck_tm = make_xn_tile(xn_tm[buf], t, ck_r, ck_negmr,
                                 res_ready[t], xn_guard[buf])
            transpose_to_fm(xn_tm[buf], ck_tm,
                            lambda c0, t=t: xnv[:, c0:c0 + 4, t * 128:(t + 1) * 128])
            xn_guard[buf] = B.ck(sPE)
        ck_xn = B.ck(sACT)
        if DEV_STOP == "xn":
            fence = phase_fence()
            ph.close()
            return res_ready, fence

        ck_lat = None
        lat_pe_done = None
        if kind == "b":
            wkv_scope = ExitStack()
            wkv_b = wkv_scope.enter_context(
                nc.sbuf_tensor("wkvbuf", [128, ND * 512], BF16))[:]
            ck_wkv = B.emit("pool", lambda g: g.dma_start(wkv_b, d_wkv_b[:, :]),
                            inc=s_bo, amt=16)
            for lc in range(4):
                for tb in range(4):
                    bank = fpool.acquire()
                    pairs = [(lambda c=c, lc=lc: wkv_b[:, c * 512 + lc * 128:
                                                       c * 512 + (lc + 1) * 128],
                              lambda c=c, tb=tb: xn_chunk(c)[:, tb * 512:(tb + 1) * 512])
                             for c in range(ND)]
                    gend = mm_group(bank, pairs, extra_waits=[ck_wkv, ck_xn])
                    eck = B.emit("act", lambda a, bank=bank, lc=lc, tb=tb:
                                 a.activation(
                        lat[:, lc * T + tb * 512:lc * T + (tb + 1) * 512],
                        bank.ap[:, 0:512], AF.Identity), waits=[gend], inc=sACT)
                    bank.readers.append(eck)
            ck_lat = B.ck(sACT)
            lat_pe_done = B.ck(sPE)
            wkv_scope.close()

        q_h = psb(f"qh{tag}", [128, T], BF16)
        k_h = psb(f"kh{tag}", [128, T], BF16)
        v_h = psb(f"vh{tag}", [128, NT * HD], BF16)
        out_h = psb(f"oh{tag}", [128, T], BF16)
        probs = [psb(f"pr{tag}{i}", [128, SL], BF16) for i in range(2)]
        probsT = [psb(f"prT{tag}{i}", [128, SL], BF16) for i in range(2)]
        dscr = sq_scr[:, 0:512]
        ones_row = sq_scr[0:1, 512:640]
        wo_b = [psb(f"wob{tag}{i}", [128, DM], BF16) for i in range(2)]
        if kind == "a":
            wqkv_b = [psb(f"wqkvb{i}", [128, ND * 384], BF16) for i in range(2)]
            bo_bc = xn_tm[0]
            bv_row = xn_tm[1][0:1, 0:DM]
            ck_bv = B.emit("pool", lambda g: g.dma_start(
                bv_row, d_bvrow[:, :]), waits=[xn_guard[0], xn_guard[1]],
                inc=s_bo, amt=16)
        else:
            wq_b = [psb(f"wqb{i}", [128, ND * 128], BF16) for i in range(2)]
            wk_b = [psb(f"wkb{i}", [128, 4 * 128], BF16) for i in range(2)]
            wv_b = [psb(f"wvb{i}", [128, 4 * 128], BF16) for i in range(2)]
            ohT = [psb(f"ohT{i}", [128, T], BF16) for i in range(2)]
            ck_bv = None

        def issue_head_dma(h, slot, w_war, o_war):
            if kind == "a":
                ckw = B.emit("pool", lambda g, h=h, slot=slot: g.dma_start(
                    wqkv_b[slot], d_wqkv[h]), waits=[w_war],
                    inc=s_wA[slot], amt=16)
            else:
                B.emit("pool", lambda g, h=h, slot=slot: g.dma_start(
                    wq_b[slot], d_wq_b[h]), waits=[w_war, lat_pe_done],
                    inc=s_wA[slot], amt=16)
                B.emit("pool", lambda g, h=h, slot=slot: g.dma_start(
                    wk_b[slot], d_wk_b[h]), inc=s_wA[slot], amt=16)
                ckw = B.emit("pool", lambda g, h=h, slot=slot: g.dma_start(
                    wv_b[slot], d_wv_b[h]), inc=s_wA[slot], amt=16)
            cko = None
            if kind == "a":
                cko = B.emit("pool", lambda g, h=h, slot=slot: g.dma_start(
                    wo_b[slot], d_wo_a[h * 128:(h + 1) * 128, :]),
                    waits=[o_war], inc=s_wO[slot], amt=16)
            return ckw, cko

        def issue_wo_pair(h0, war):
            B.emit("pool", lambda g, h0=h0: g.dma_start(
                wo_b[0], d_wo_b[h0 * 128:(h0 + 1) * 128, :]), waits=[war],
                inc=s_wO[0], amt=16)
            o_cks[0] = B.ck(s_wO[0])
            B.emit("pool", lambda g, h0=h0: g.dma_start(
                wo_b[1], d_wo_b[(h0 + 1) * 128:(h0 + 2) * 128, :]),
                inc=s_wO[1], amt=16)
            o_cks[1] = B.ck(s_wO[1])

        def ktiles(j):
            if kind == "a":
                return [(kt, None) for kt in range(j + 1)]
            if j == 0:
                return [(0, 1)]
            if j == 1:
                return [(0, 4), (1, 1)]
            return [(0, 3), (j - 1, 2), (j, 1)]

        w_cks = [None, None]
        o_cks = [None, None]
        qkv_consumed = [None, None]
        wo_consumed = [None, None]
        w_cks[0], _cko = issue_head_dma(0, 0, None, None)
        if _cko is not None:
            o_cks[0] = _cko

        probs_war = [None, None]
        probsT_war = [None, None]
        ohT_war = [None, None]
        dscr_war = None
        head_guard = None
        ck_ones = B.emit("dve", lambda v: v.tensor_scalar(
            ones_row, ident[0:1, 0:128], 0.0, 1.0, ALU.mult, ALU.add),
            waits=[ck_consts], inc=sDVE)
        if kind == "b":
            issue_wo_pair(0, None)
            ohT_cks = [[None] * NT, [None] * NT]
        for h in range(DEV_HEADS):
            slot = h % 2
            if h + 1 < DEV_HEADS:
                ns = (h + 1) % 2
                w_cks[ns], _cko = issue_head_dma(
                    h + 1, ns, qkv_consumed[ns], wo_consumed[ns])
                if _cko is not None:
                    o_cks[ns] = _cko

            if kind == "a":
                wv4 = wqkv_b[slot].rearrange("p (c s w) -> p c s w", c=ND, s=3)
                qk_specs = [(0, q_h, h), (1, k_h, 16 + h)]
            else:
                qk_specs = [(None, q_h, None), (None, k_h, None)]
            for si, (sel, dst, bc) in enumerate(qk_specs):
                for tb in range(4):
                    bank = fpool.acquire()
                    if kind == "a":
                        pairs = [(lambda c=c, sel=sel, wv4=wv4: wv4[:, c, sel, :],
                                  lambda c=c, tb=tb: xn_chunk(c)[:, tb * 512:(tb + 1) * 512])
                                 for c in range(ND)]
                        ew = [w_cks[slot], ck_xn, ck_consts]
                    elif si == 0:
                        pairs = [(lambda c=c, slot=slot: wq_b[slot][:, c * 128:(c + 1) * 128],
                                  lambda c=c, tb=tb: xn_chunk(c)[:, tb * 512:(tb + 1) * 512])
                                 for c in range(ND)]
                        ew = [w_cks[slot], ck_xn]
                    else:
                        pairs = [(lambda lc=lc, slot=slot: wk_b[slot][:, lc * 128:(lc + 1) * 128],
                                  lambda lc=lc, tb=tb: lat[:, lc * T + tb * 512:
                                                           lc * T + (tb + 1) * 512])
                                 for lc in range(4)]
                        ew = [w_cks[slot], ck_lat]
                    gend = mm_group(bank, pairs, extra_waits=ew)
                    if bc is not None:
                        eck = B.emit("act", lambda a, bank=bank, dst=dst, tb=tb,
                                     bc=bc: a.activation(
                            dst[:, tb * 512:(tb + 1) * 512], bank.ap[:, 0:512],
                            AF.Identity, bias=bqkv[:, bc:bc + 1]),
                            waits=[gend], inc=sACT)
                    else:
                        eck = B.emit("act", lambda a, bank=bank, dst=dst, tb=tb:
                                     a.activation(
                            dst[:, tb * 512:(tb + 1) * 512], bank.ap[:, 0:512],
                            AF.Identity), waits=[gend], inc=sACT)
                    bank.readers.append(eck)

            for tg in range(4):
                bank = fpool.acquire()
                fpool.pe_waits(bank)
                gend = None
                nk = ND if kind == "a" else 4
                for i in range(4):
                    tt = tg * 4 + i
                    last_c = (kind != "a")
                    for c in range(nk):
                        if kind == "a":
                            fn = (lambda p, c=c, tt=tt, i=i, bank=bank, wv4=wv4,
                                  nk=nk: p.matmul(
                                bank.ap[:, i * 128:(i + 1) * 128],
                                xn_chunk(c)[:, tt * 128:(tt + 1) * 128],
                                wv4[:, c, 2, :], start=(c == 0),
                                stop=False, skip_group_check=True))
                            ew = [w_cks[slot], ck_xn]
                        else:
                            fn = (lambda p, c=c, tt=tt, i=i, bank=bank,
                                  slot=slot, nk=nk: p.matmul(
                                bank.ap[:, i * 128:(i + 1) * 128],
                                lat[:, c * T + tt * 128:c * T + (tt + 1) * 128],
                                wv_b[slot][:, c * 128:(c + 1) * 128],
                                start=(c == 0), stop=(c == nk - 1),
                                skip_group_check=True))
                            ew = [w_cks[slot], ck_lat]
                        gend = B.emit("pe", fn,
                                      waits=ew if (c == 0 and i == 0) else [],
                                      inc=sPE if (c == nk - 1 and i == 3
                                                  and last_c) else None)
                    if kind == "a":
                        gend = B.emit("pe", lambda p, i=i, h=h, bank=bank:
                                      p.matmul(
                            bank.ap[:, i * 128:(i + 1) * 128],
                            ones_row,
                            bv_row[0:1, h * 128:(h + 1) * 128],
                            start=False, stop=True, skip_group_check=True),
                            waits=[ck_bv, ck_ones] if (tg == 0 and i == 0)
                            else [],
                            inc=sPE if i == 3 else None)
                bank.writer = gend
                eck = B.emit("act", lambda a, bank=bank, tg=tg: a.activation(
                    v_h[:, tg * 512:(tg + 1) * 512], bank.ap[:, 0:512],
                    AF.Identity), waits=[gend], inc=sACT)
                bank.readers.append(eck)
            ck_qkv = B.ck(sACT)
            if DEV_STOP == "qkv":
                continue

            out_h_cks = [None] * NT
            sums_cols = []
            recip_cks = [None] * NT
            st1 = None
            st2 = None
            st3 = None
            for j in range(NT + (2 if kind == "a" else 3)):
                nst = None
                if j < NT:
                    tj = ktiles(j)
                    tiles_j = [kt for kt, _ in tj]
                    L = len(tj) * 128
                    par = j % 2
                    pbuf = probs[par]
                    if kind == "a":
                        nchunks = ceil_div(L, 512)
                        cnt = 0
                        for kc in range(nchunks):
                            n = min(512, L - kc * 512)
                            bank = fpool.acquire()
                            gend = mm_group(bank, [(
                                lambda j=j: q_h[:, j * 128:(j + 1) * 128],
                                lambda kc=kc, n=n: k_h[:, kc * 512:kc * 512 + n])],
                                extra_waits=[ck_qkv], n_slice=n)
                            ndiag = n - 128 if kc == nchunks - 1 else n
                            eck_chunk = None
                            if ndiag > 0:
                                eck_chunk = B.emit(
                                    "act", lambda a, bank=bank, kc=kc, nd=ndiag,
                                    pbuf=pbuf, j=j, cnt=cnt: a.activation(
                                        pbuf[:, kc * 512:kc * 512 + nd],
                                        bank.ap[:, 0:nd], AF.Exp, scale=SCALE,
                                        accum_out=gsc[:, j * 5 + cnt:
                                                      j * 5 + cnt + 1]),
                                    waits=[gend, probs_war[par], head_guard],
                                    inc=sACT)
                                bank.readers.append(eck_chunk)
                                cnt += 1
                            if kc == nchunks - 1:
                                dvw = [gend, ck_consts, dscr_war]
                                if eck_chunk is not None:
                                    dvw.append(eck_chunk)
                                dck = B.emit("dve", lambda v, bank=bank, n=n:
                                             v.scalar_tensor_tensor(
                                    dscr[:, 0:128], bank.ap[:, n - 128:n],
                                    SCALE, masks[:, 0:128], ALU.mult, ALU.add),
                                    waits=dvw, inc=sDVE)
                                bank.readers.append(dck)
                                eck = B.emit("act", lambda a, pbuf=pbuf, L=L,
                                             j=j, cnt=cnt: a.activation(
                                    pbuf[:, L - 128:L], dscr[:, 0:128], AF.Exp,
                                    accum_out=gsc[:, j * 5 + cnt:
                                                  j * 5 + cnt + 1]),
                                    waits=[dck, probs_war[par], head_guard],
                                    inc=sACT)
                                dscr_war = eck
                                cnt += 1
                        sums_cols.append((j, cnt))
                    else:
                        bank = fpool.acquire()
                        fpool.pe_waits(bank)
                        gend = None
                        for i, (kt, mid) in enumerate(tj):
                            gend = B.emit("pe", lambda p, kt=kt, i=i, j=j,
                                          bank=bank: p.matmul(
                                bank.ap[:, i * 128:(i + 1) * 128],
                                q_h[:, j * 128:(j + 1) * 128],
                                k_h[:, kt * 128:(kt + 1) * 128],
                                start=True, stop=True, skip_group_check=True),
                                waits=[ck_qkv] if i == 0 else [],
                                inc=sPE if i == len(tj) - 1 else None)
                        bank.writer = gend
                        dcks = []
                        for i, (kt, mid) in enumerate(tj):
                            dck = B.emit("dve", lambda v, bank=bank, i=i,
                                         mid=mid: v.scalar_tensor_tensor(
                                dscr[:, i * 128:(i + 1) * 128],
                                bank.ap[:, i * 128:(i + 1) * 128],
                                SCALE, masks[:, mid * 128:(mid + 1) * 128],
                                ALU.mult, ALU.add),
                                waits=[gend, ck_consts, dscr_war], inc=sDVE)
                            bank.readers.append(dck)
                            dcks.append(dck)
                        eck = B.emit("act", lambda a, pbuf=pbuf, L=L, j=j:
                                     a.activation(
                            pbuf[:, 0:L], dscr[:, 0:L], AF.Exp,
                            accum_out=gsc[:, j:j + 1]),
                            waits=dcks + [probs_war[par], head_guard], inc=sACT)
                        dscr_war = eck
                        recip_cks[j] = B.emit("dve", lambda v, j=j:
                                              v.reciprocal(
                            stats[:, j:j + 1], gsc[:, j:j + 1]),
                            waits=[eck], inc=sDVE)
                    nst = (j, tiles_j, L, eck)

                nst2 = None
                if st1 is not None:
                    pj, ptiles, pL, pev = st1
                    ppar = pj % 2
                    pbuf2 = probs[ppar]
                    ptb = probsT[ppar]
                    ntj = len(ptiles)
                    ev_cks = []
                    for g0 in range(0, ntj, 4):
                        gn = min(4, ntj - g0)
                        bank2 = bpool.acquire()
                        bpool.pe_waits(bank2)
                        gend2 = None
                        for i in range(gn):
                            gend2 = B.emit("pe", lambda p, g0=g0, i=i,
                                           bank2=bank2, pbuf2=pbuf2:
                                           p.transpose(
                                bank2.ap[:, i * 128:(i + 1) * 128],
                                pbuf2[:, (g0 + i) * 128:(g0 + i + 1) * 128],
                                ident),
                                waits=[pev] if (g0 == 0 and i == 0) else [],
                                inc=sPE if i == gn - 1 else None)
                        bank2.writer = gend2
                        ev = B.emit("dve", lambda v, bank2=bank2, g0=g0, gn=gn,
                                    ptb=ptb: v.tensor_scalar(
                            ptb[:, g0 * 128:(g0 + gn) * 128],
                            bank2.ap[:, 0:gn * 128], 1.0, None, ALU.mult),
                            waits=[gend2, probsT_war[ppar]], inc=sDVE)
                        bank2.readers.append(ev)
                        ev_cks.append(ev)
                    probs_war[ppar] = B.ck(sPE)
                    nst2 = (pj, ptiles, ev_cks)

                nst3 = None
                if st2 is not None:
                    ppj, pptiles, pev_cks = st2
                    apar = ppj % 2
                    ptb2 = probsT[apar]
                    bank3 = fpool.acquire()
                    if kind == "a":
                        pairs = [(lambda kt=kt: v_h[:, kt * 128:(kt + 1) * 128],
                                  lambda i=i, ptb2=ptb2: ptb2[:, i * 128:(i + 1) * 128])
                                 for i, kt in enumerate(pptiles)]
                    else:
                        pairs = [(lambda i=i, ptb2=ptb2: ptb2[:, i * 128:(i + 1) * 128],
                                  lambda kt=kt: v_h[:, kt * 128:(kt + 1) * 128])
                                 for i, kt in enumerate(pptiles)]
                    gend3 = mm_group(bank3, pairs, extra_waits=pev_cks,
                                     n_slice=128)
                    if kind == "a":
                        oek = B.emit("act", lambda a, bank3=bank3, ppj=ppj:
                                     a.activation(
                            out_h[:, ppj * 128:(ppj + 1) * 128],
                            bank3.ap[:, 0:128], AF.Identity),
                            waits=[gend3], inc=sACT)
                    else:
                        oek = B.emit("act", lambda a, bank3=bank3, ppj=ppj:
                                     a.activation(
                            out_h[:, ppj * 128:(ppj + 1) * 128],
                            bank3.ap[:, 0:128], AF.Identity,
                            scale=stats[:, ppj:ppj + 1]),
                            waits=[gend3, recip_cks[ppj]], inc=sACT)
                        nst3 = (ppj, oek)
                    bank3.readers.append(oek)
                    out_h_cks[ppj] = oek
                    probsT_war[apar] = B.ck(sPE)

                if kind == "b" and st3 is not None:
                    qj, qck = st3
                    bank4 = bpool.acquire()
                    bpool.pe_waits(bank4)
                    gend4 = B.emit("pe", lambda p, qj=qj, bank4=bank4:
                                   p.transpose(
                        bank4.ap[:, 0:128],
                        out_h[:, qj * 128:(qj + 1) * 128], ident),
                        waits=[qck], inc=sPE)
                    bank4.writer = gend4
                    ohe = B.emit("act", lambda a, qj=qj, bank4=bank4, hp=h % 2:
                                 a.activation(
                        ohT[hp][:, qj * 128:(qj + 1) * 128],
                        bank4.ap[:, 0:128], AF.Identity),
                        waits=[gend4, ohT_war[h % 2]], inc=sACT)
                    bank4.readers.append(ohe)
                    ohT_cks[h % 2][qj] = ohe
                st3 = nst3
                st2 = nst2
                st1 = nst

            qkv_consumed[slot] = B.ck(sPE)

            if kind == "a":
                ck_allact = B.ck(sACT)
                for (j2, cnt2) in sums_cols:
                    B.emit("dve", lambda v, j2=j2, cnt2=cnt2: v.reduce_sum(
                        stats[:, 16 + j2:17 + j2],
                        gsc[:, j2 * 5:j2 * 5 + cnt2], AX), waits=[ck_allact])
                B.drain("dve")
                head_guard = B.emit("dve", lambda v: v.reciprocal(
                    stats[:, 0:16], stats[:, 16:32]),
                    waits=[ck_allact], inc=sDVE)
                B.drain("dve")

                for tt in range(NT):
                    last = None
                    for od in range(4):
                        bank = fpool.acquire()
                        gend = mm_group(bank, [(
                            lambda tt=tt: out_h[:, tt * 128:(tt + 1) * 128],
                            lambda od=od, slot=slot: wo_b[slot][:, od * 512:(od + 1) * 512])],
                            extra_waits=[out_h_cks[tt], o_cks[slot]])
                        last = B.emit("dve", lambda v, bank=bank, tt=tt, od=od:
                                      v.scalar_tensor_tensor(
                            res_slice(tt, od * 512, (od + 1) * 512),
                            bank.ap[:, 0:512], stats[:, tt:tt + 1],
                            res_slice(tt, od * 512, (od + 1) * 512),
                            ALU.mult, ALU.add),
                            waits=[gend], inc=sDVE)
                        bank.readers.append(last)
                    res_ready[tt] = last
                wo_consumed[slot] = B.ck(sPE)
            else:
                head_guard = B.ck(sDVE)
                if h % 2 == 1:
                    for tt in range(NT):
                        last = None
                        for od in range(4):
                            bank = fpool.acquire()
                            pairs = [(lambda tt=tt, pp=pp: ohT[pp][:, tt * 128:(tt + 1) * 128],
                                      lambda od=od, pp=pp: wo_b[pp][:, od * 512:(od + 1) * 512])
                                     for pp in range(2)]
                            gend = mm_group(bank, pairs, extra_waits=[
                                ohT_cks[0][tt], ohT_cks[1][tt],
                                o_cks[0], o_cks[1]])
                            last = B.emit("dve", lambda v, bank=bank, tt=tt,
                                          od=od: v.tensor_tensor(
                                res_slice(tt, od * 512, (od + 1) * 512),
                                bank.ap[:, 0:512],
                                res_slice(tt, od * 512, (od + 1) * 512),
                                ALU.add), waits=[gend], inc=sDVE)
                            bank.readers.append(last)
                        res_ready[tt] = last
                    wo_consumed[0] = wo_consumed[1] = B.ck(sPE)
                    ohT_war[0] = ohT_war[1] = B.ck(sPE)
                    if h + 2 < DEV_HEADS:
                        issue_wo_pair(h + 1, B.ck(sPE))

        if kind == "a":
            ck_bo = B.emit("pool", lambda g: g.dma_start(bo_bc, d_bo[:, :]),
                           waits=[xn_guard[0]], inc=s_bo, amt=16)
            for tt in range(NT):
                res_ready[tt] = B.emit("dve", lambda v, tt=tt: v.tensor_tensor(
                    res_tile(tt), res_tile(tt), bo_bc, ALU.add),
                    waits=[ck_bo, res_ready[tt]], inc=sDVE)

        fence = phase_fence()
        ph.close()
        return res_ready, fence

    moe_ctr = [0]

    def moe_phase(res_ready, final):
        ph = ExitStack()
        sfx = str(moe_ctr[0])
        moe_ctr[0] += 1

        def psb(name, shape, dt):
            return ph.enter_context(nc.sbuf_tensor(name + sfx, shape, dt))[:]

        TH = T // 2
        xn = psb("xnm", [128, ND * TH], BF16)
        xn_tm = [psb(f"xntmm{i}", [128, DM], BF16) for i in range(2)]
        Hbuf = psb("Hm", [128, NI * TH], BF16)
        w1b = [psb(f"w1b{i}", [128, ND * 128], BF16) for i in range(3)]
        w3b = [psb(f"w3b{i}", [128, ND * 128], BF16) for i in range(3)]
        w2b = [psb(f"w2b{i}", [128, NI * 512], BF16) for i in range(2)]
        sA = [psb(f"sAb{i}", [128, 512], BF16) for i in range(2)]
        ostage = [psb(f"ost{i}", [128, DM], F32) for i in range(2)] \
            if final else [None, None]

        def xn_chunk(c):
            return xn[:, c * TH:(c + 1) * TH]

        xnv = xn.rearrange("p (c w) -> p c w", c=ND)
        ck_r, ck_negmr = ln_stats(res_ready)

        def wsrc(u):
            if u < 2:
                return (lambda ic, u=u: d_sw1[u, ic],
                        lambda ic, u=u: d_sw3[u, ic],
                        lambda dq, u=u: d_sw2[u, dq], None)
            e = u - 2
            return (lambda ic, e=e: d_w1[e, ic],
                    lambda ic, e=e: d_w3[e, ic],
                    lambda dq, e=e: d_w2[e, dq], e)

        w13_consumed = [None, None, None]
        w2_consumed = [None, None]
        sA_war = [None, None]
        w13_ready = {}
        w13_slot = {}
        w2_ready = {}
        w2_slot = {}
        ctr = {"w13": 0, "w2": 0}

        def issue_w13(half, u, ic):
            slot = ctr["w13"] % 3
            ctr["w13"] += 1
            s1, s3, _, _ = wsrc(u)
            B.emit("pool", lambda g, s1=s1, ic=ic, slot=slot: g.dma_start(
                w1b[slot], s1(ic)), waits=[w13_consumed[slot]],
                inc=s_w13[slot], amt=16)
            ck = B.emit("pool", lambda g, s3=s3, ic=ic, slot=slot: g.dma_start(
                w3b[slot], s3(ic)), inc=s_w13[slot], amt=16)
            w13_ready[(half, u, ic)] = ck
            w13_slot[(half, u, ic)] = slot

        def issue_w2(half, u, dq):
            slot = ctr["w2"] % 2
            ctr["w2"] += 1
            _, _, s2, _ = wsrc(u)
            ck = B.emit("pool", lambda g, s2=s2, dq=dq, slot=slot: g.dma_start(
                w2b[slot], s2(dq)), waits=[w2_consumed[slot]],
                inc=s_w2[slot], amt=16)
            w2_ready[(half, u, dq)] = ck
            w2_slot[(half, u, dq)] = slot

        res_done = list(res_ready)
        out_cks = []
        ostage_free = [None, None]
        dump_xn_ck = [None]
        dump_h_pending = [None]
        w13_seq = [(hf, u, ic) for hf in range(2)
                   for u in range(NU) for ic in range(NI)]
        w2_seq = [(hf, u, dq) for hf in range(2)
                  for u in range(NU) for dq in range(4)]
        ptrs = {"w13": 0, "w2": 0}

        def pump_w13():
            if ptrs["w13"] < len(w13_seq):
                issue_w13(*w13_seq[ptrs["w13"]])
                ptrs["w13"] += 1

        def pump_w2():
            if ptrs["w2"] < len(w2_seq):
                issue_w2(*w2_seq[ptrs["w2"]])
                ptrs["w2"] += 1

        for _ in range(3):
            pump_w13()
        for _ in range(2):
            pump_w2()
        for half in range(2):
            t0 = half * 8
            if half == 1 and dump_xn_ck[0] is not None:
                B.wait("act", dump_xn_ck[0])
            xn_guard = [None, None]
            for tl in range(8):
                t = t0 + tl
                buf = tl % 2
                ck_tm = make_xn_tile(xn_tm[buf], t, ck_r, ck_negmr,
                                     res_done[t], xn_guard[buf])
                transpose_to_fm(
                    xn_tm[buf], ck_tm,
                    lambda c0, tl=tl: xnv[:, c0:c0 + 4, tl * 128:(tl + 1) * 128])
                xn_guard[buf] = B.ck(sPE)
            ck_xn = B.ck(sACT)
            if DEV_MOE_DUMP and half == 0:
                dump_xn_ck[0] = B.emit("sync", lambda s: s.dma_start(
                    d_mxn[:, :], xn), waits=[ck_xn], inc=sS, amt=16)

            for tl in range(8):
                t = t0 + tl
                bank = fpool.acquire()
                pairs = [(lambda c=c, tl=tl: xn_chunk(c)[:, tl * 128:(tl + 1) * 128],
                          lambda c=c: gw[:, c * 16:(c + 1) * 16])
                         for c in range(ND)]
                gend = mm_group(bank, pairs, extra_waits=[ck_xn, ck_gw],
                                n_slice=16)
                ck_gl = B.emit("act", lambda a, bank=bank: a.activation(
                    gsc[:, 0:16], bank.ap[:, 0:16], AF.Identity),
                    waits=[gend], inc=sACT)
                bank.readers.append(ck_gl)
                ck_gm = B.emit("dve", lambda v: v.reduce_max(
                    tiny[:, 4:5], gsc[:, 0:16], AX, negate=True),
                    waits=[ck_gl], inc=sDVE)
                ck_ge = B.emit("act", lambda a: a.activation(
                    gsc[:, 16:32], gsc[:, 0:16], AF.Exp,
                    bias=tiny[:, 4:5], accum_out=tiny[:, 5:6]),
                    waits=[ck_gm], inc=sACT)
                B.emit("dve", lambda v: v.reciprocal(tiny[:, 6:7], tiny[:, 5:6]),
                       waits=[ck_ge])
                B.drain("dve")
                B.emit("dve", lambda v: v.tensor_scalar(
                    gsc[:, 32:48], gsc[:, 16:32], tiny[:, 6:7], None, ALU.mult))
                B.drain("dve")
                B.emit("dve", lambda v: v.reduce_max(
                    tiny[:, 4:5], gsc[:, 32:48], AX))
                B.drain("dve")
                B.emit("dve", lambda v: v.tensor_scalar(
                    gsc[:, 48:64], gsc[:, 32:48], tiny[:, 4:5], None, ALU.is_lt))
                B.drain("dve")
                B.emit("dve", lambda v: v.tensor_tensor(
                    gsc[:, 64:80], gsc[:, 32:48], gsc[:, 48:64], ALU.mult))
                B.drain("dve")
                B.emit("dve", lambda v: v.reduce_max(
                    tiny[:, 5:6], gsc[:, 64:80], AX))
                B.drain("dve")
                B.emit("dve", lambda v: v.tensor_scalar(
                    gsc[:, 48:64], gsc[:, 32:48], tiny[:, 5:6], None, ALU.is_ge))
                B.drain("dve")
                B.emit("dve", lambda v, t=t: v.tensor_tensor(
                    wgate[:, t * 16:(t + 1) * 16], gsc[:, 32:48],
                    gsc[:, 48:64], ALU.mult), inc=sDVE)
                B.drain("dve")
            ck_gate = B.ck(sDVE)
            if DEV_MOE_DUMP and half == 0:
                B.emit("sync", lambda s: s.dma_start(d_mwg[:, :],
                                                     wgate[:, 0:256]),
                       waits=[ck_gate], inc=sS, amt=16)

            for u in range(NU):
                _, _, _, e = wsrc(u)
                if dump_h_pending[0] is not None:
                    B.wait("dve", dump_h_pending[0])
                    dump_h_pending[0] = None
                for ic in range(NI):
                    slot = w13_slot[(half, u, ic)]
                    wck = w13_ready[(half, u, ic)]
                    for tb in range(2):
                        bankA = fpool.acquire()
                        pairs = [(lambda c=c, slot=slot: w1b[slot][:, c * 128:(c + 1) * 128],
                                  lambda c=c, tb=tb: xn_chunk(c)[:, tb * 512:(tb + 1) * 512])
                                 for c in range(ND)]
                        ckA = mm_group(bankA, pairs, extra_waits=[wck, ck_xn])
                        bankB = fpool.acquire()
                        pairs = [(lambda c=c, slot=slot: w3b[slot][:, c * 128:(c + 1) * 128],
                                  lambda c=c, tb=tb: xn_chunk(c)[:, tb * 512:(tb + 1) * 512])
                                 for c in range(ND)]
                        ckB = mm_group(bankB, pairs)
                        sbuf = sA[tb]
                        ck_s = B.emit("act", lambda a, bankA=bankA, sbuf=sbuf:
                                      a.activation(sbuf, bankA.ap[:, 0:512],
                                                   AF.Silu),
                                      waits=[ckA, sA_war[tb]], inc=sACT)
                        bankA.readers.append(ck_s)
                        ck_h = B.emit("dve", lambda v, bankB=bankB, sbuf=sbuf,
                                      ic=ic, tb=tb: v.tensor_tensor(
                            Hbuf[:, ic * TH + tb * 512:ic * TH + (tb + 1) * 512],
                            sbuf, bankB.ap[:, 0:512], ALU.mult),
                            waits=[ckB, ck_s], inc=sDVE)
                        bankB.readers.append(ck_h)
                        sA_war[tb] = ck_h
                    w13_consumed[slot] = B.ck(sPE)
                    pump_w13()
                ck_H = B.ck(sDVE)
                if DEV_MOE_DUMP and half == 0 and u in (0, 2):
                    dump_h_pending[0] = B.emit(
                        "sync", lambda s, u=u: s.dma_start(
                            d_mh[u][:, :], Hbuf), waits=[ck_H], inc=sS, amt=16)

                for dq in range(4):
                    slot = w2_slot[(half, u, dq)]
                    wck = w2_ready[(half, u, dq)]
                    for tl in range(8):
                        t = t0 + tl
                        bank = fpool.acquire()
                        pairs = [(lambda ic=ic, tl=tl: Hbuf[:, ic * TH + tl * 128:
                                                            ic * TH + (tl + 1) * 128],
                                  lambda ic=ic, slot=slot: w2b[slot][:, ic * 512:
                                                                     (ic + 1) * 512])
                                 for ic in range(NI)]
                        gend = mm_group(bank, pairs,
                                        extra_waits=[wck, ck_H, ck_gate])
                        if e is None:
                            eck = B.emit("dve", lambda v, bank=bank, t=t, dq=dq:
                                         v.tensor_tensor(
                                res_slice(t, dq * 512, (dq + 1) * 512),
                                bank.ap[:, 0:512],
                                res_slice(t, dq * 512, (dq + 1) * 512), ALU.add),
                                waits=[gend], inc=sDVE)
                        else:
                            eck = B.emit("dve", lambda v, bank=bank, t=t, dq=dq,
                                         e=e: v.scalar_tensor_tensor(
                                res_slice(t, dq * 512, (dq + 1) * 512),
                                bank.ap[:, 0:512],
                                wgate[:, t * 16 + e:t * 16 + e + 1],
                                res_slice(t, dq * 512, (dq + 1) * 512),
                                ALU.mult, ALU.add),
                                waits=[gend], inc=sDVE)
                        bank.readers.append(eck)
                        if u == NU - 1:
                            res_done[t] = eck
                            if final and dq == 3:
                                obuf = ostage[tl % 2]
                                cko = B.emit("act", lambda a, obuf=obuf, t=t:
                                             a.activation(obuf, res_tile(t),
                                                          AF.Identity),
                                             waits=[eck, ostage_free[tl % 2]],
                                             inc=sACT)
                                dck = B.emit("sync", lambda s, obuf=obuf, t=t:
                                             s.dma_start(
                                    dy[t * 128:(t + 1) * 128, :], obuf),
                                    waits=[cko], inc=sOUT[tl % 2], amt=16)
                                out_cks.append(dck)
                                ostage_free[tl % 2] = dck
                    w2_consumed[slot] = B.ck(sPE)
                    pump_w2()

        fence = phase_fence()
        ph.close()
        return res_done, fence, out_cks

    cur = res_ready
    out_cks = []
    fence = None
    for i, sl in enumerate(sublayers):
        if fence is not None:
            apply_fence(fence)
        final = (i == len(sublayers) - 1)
        if sl.startswith("attn"):
            cur, fence = attn_phase("a" if sl == "attn_a" else "b", cur)
            if final:
                obuf = es.enter_context(
                    nc.sbuf_tensor("odbg", [128, DM], F32))[:]
                prev = None
                for t in range(NT):
                    cko = B.emit("act", lambda a, t=t, obuf=obuf: a.activation(
                        obuf, res_tile(t), AF.Identity),
                        waits=[cur[t], prev], inc=sACT)
                    prev = B.emit("sync", lambda s, t=t, obuf=obuf: s.dma_start(
                        dy[t * 128:(t + 1) * 128, :], obuf),
                        waits=[cko], inc=sOUT[t % 2], amt=16)
                    out_cks.append(prev)
        else:
            cur, fence, ocks = moe_phase(cur, final)
            out_cks.extend(ocks)
            if final and not ocks:
                raise RuntimeError("final moe produced no outputs")

    B.wait("sync", (sOUT[0], sOUT[0].n))
    B.wait("sync", (sOUT[1], sOUT[1].n))

    with nc.Block() as block:
        @block.tensor
        def _(eng):
            for f in B.ops["pe"]:
                f(eng)

        @block.scalar
        def _(eng):
            for f in B.ops["act"]:
                f(eng)

        @block.vector
        def _(eng):
            for f in B.ops["dve"]:
                f(eng)

        @block.gpsimd
        def _(eng):
            for f in B.ops["pool"]:
                f(eng)

        @block.sync
        def _(eng):
            for f in B.ops["sync"]:
                f(eng)

    es.close()
    return nc


def _make_masks():
    q = np.arange(128)[:, None]
    k = np.arange(128)[None, :]
    causal_a = np.where(k <= q, 0.0, -10000.0)
    causal_b = np.where(k <= q, 0.0, -60000.0)
    upper_b = np.where(k > q, 0.0, -60000.0)
    persist_b = np.where(np.broadcast_to(k < 16, (128, 128)), 0.0, -60000.0)
    union_b = np.where((k > q) | (k < 16), 0.0, -60000.0)
    m = np.concatenate([causal_a, causal_b, upper_b, persist_b, union_b],
                       axis=1).astype(ml_dtypes.bfloat16)
    return np.ascontiguousarray(m)


def _prep_shared(Wqkv, bqkv, Wo_a, bo, Wq, Wkv, Wk, Wv, Wo_b,
                 gate_w, w1, w2, w3, sw1, sw2, sw3):
    f32 = np.float32
    c = np.ascontiguousarray
    Wqkv = np.asarray(Wqkv, f32)
    wqkvh = c(Wqkv.reshape(16, 128, 3, 16, 128).transpose(3, 1, 0, 2, 4)
              .reshape(NH, 128, ND * 384))
    wq = np.asarray(Wq, f32)
    wqh = c(wq.reshape(16, 128, 16, 128).transpose(2, 1, 0, 3)
            .reshape(NH, 128, ND * 128))
    wk = np.asarray(Wk, f32)
    wkh = c(wk.reshape(4, 128, 16, 128).transpose(2, 1, 0, 3)
            .reshape(NH, 128, 4 * 128))
    wv = np.asarray(Wv, f32)
    wvh = c(wv.reshape(4, 128, 16, 128).transpose(2, 1, 0, 3)
            .reshape(NH, 128, 4 * 128))
    wkvb = c(np.asarray(Wkv, f32).reshape(16, 128, 512).transpose(1, 0, 2)
             .reshape(128, ND * 512))
    gwc = c(np.asarray(gate_w, f32).T.reshape(16, 128, 16).transpose(1, 0, 2)
            .reshape(128, ND * 16))
    w1 = np.asarray(w1, f32)
    w1c = c(w1.transpose(0, 2, 1).reshape(16, 16, 128, 11, 128)
            .transpose(0, 3, 2, 1, 4).reshape(16, NI, 128, ND * 128))
    w3 = np.asarray(w3, f32)
    w3c = c(w3.transpose(0, 2, 1).reshape(16, 16, 128, 11, 128)
            .transpose(0, 3, 2, 1, 4).reshape(16, NI, 128, ND * 128))
    w2 = np.asarray(w2, f32)
    w2c = c(w2.transpose(0, 2, 1).reshape(16, 11, 128, 4, 512)
            .transpose(0, 3, 2, 1, 4).reshape(16, 4, 128, NI * 512))
    sw1 = np.asarray(sw1, f32)
    sw1c = c(sw1.T.reshape(16, 128, 2, 11, 128).transpose(2, 3, 1, 0, 4)
             .reshape(2, NI, 128, ND * 128))
    sw3 = np.asarray(sw3, f32)
    sw3c = c(sw3.T.reshape(16, 128, 2, 11, 128).transpose(2, 3, 1, 0, 4)
             .reshape(2, NI, 128, ND * 128))
    sw2 = np.asarray(sw2, f32)
    sw2c = c(sw2.T.reshape(2, 11, 128, 4, 512).transpose(0, 3, 2, 1, 4)
             .reshape(2, 4, 128, NI * 512))
    return {
        "wqkvh": wqkvh,
        "bqkvt": c(np.asarray(bqkv, f32).reshape(48, 128).T),
        "bvrow": c(np.asarray(bqkv, f32)[2 * DM:3 * DM].reshape(1, DM)),
        "wo_a": c(np.asarray(Wo_a, f32)),
        "bo_bc": c(np.broadcast_to(np.asarray(bo, f32)[None, :], (128, DM))),
        "wqh": wqh, "wkvb": wkvb, "wkh": wkh, "wvh": wvh,
        "wo_b": c(np.asarray(Wo_b, f32)),
        "gwc": gwc, "w1c": w1c, "w3c": w3c, "w2c": w2c,
        "sw1c": sw1c, "sw3c": sw3c, "sw2c": sw2c,
        "ident": np.eye(128, dtype=ml_dtypes.bfloat16),
        "masks": _make_masks(),
    }


_PROG_CACHE = {}


def _get_prog(subl=SUBLAYERS):
    key = tuple(subl)
    if key not in _PROG_CACHE:
        _PROG_CACHE[key] = build_program(subl)
    return _PROG_CACHE[key]


def kernel(L0_attn_a_x, L0_attn_a_Wqkv, L0_attn_a_bqkv, L0_attn_a_Wo,
           L0_attn_a_bo, L0_attn_b_Wq, L0_attn_b_Wkv, L0_attn_b_Wk,
           L0_attn_b_Wv, L0_attn_b_Wo, L0_ffn_gate_w, L0_ffn_w1, L0_ffn_w2,
           L0_ffn_w3, L0_ffn_sw1, L0_ffn_sw2, L0_ffn_sw3, _subl=SUBLAYERS):
    shared = _prep_shared(L0_attn_a_Wqkv, L0_attn_a_bqkv, L0_attn_a_Wo,
                          L0_attn_a_bo, L0_attn_b_Wq, L0_attn_b_Wkv,
                          L0_attn_b_Wk, L0_attn_b_Wv, L0_attn_b_Wo,
                          L0_ffn_gate_w, L0_ffn_w1, L0_ffn_w2, L0_ffn_w3,
                          L0_ffn_sw1, L0_ffn_sw2, L0_ffn_sw3)
    x = np.asarray(L0_attn_a_x, np.float32)
    nc = _get_prog(_subl)
    in_maps = []
    for i in range(8):
        m = dict(shared)
        m["x"] = np.ascontiguousarray(x[i])
        in_maps.append(m)
    out = run_bass_kernel_spmd(nc, in_maps, core_ids=list(range(8)))
    global LAST_RESULTS
    LAST_RESULTS = out.results
    return np.stack([r["y"] for r in out.results], axis=0)


LAST_RESULTS = None
